# version 23
# baseline (speedup 1.0000x reference)
import numpy as np
import ml_dtypes
import contextlib

import concourse.bass as bass
import concourse.mybir as mybir
from concourse.bass_utils import run_bass_kernel_spmd

F32 = mybir.dt.float32
BF16 = mybir.dt.bfloat16
AF = mybir.ActivationFunctionType
ALU = mybir.AluOpType
AX = mybir.AxisListType
bfloat16 = ml_dtypes.bfloat16

T, D, H = 4096, 1024, 4
DKH = D // H
M = 64
HM = H * M
E = 8
KB = D // 128
NPASS = 4
QT = T // NPASS
CH = 128
NCH = QT // CH
TIL = 512
NTIL = QT // TIL
NSUB = TIL // CH
ISH = 2048
NEP = E + 2
LN_EPS = 1e-5
GN_EPS = 1e-5
ATT_SCALE = 0.0625
CLAMP = 32.0

ENGS = ("pe", "dve", "act", "pool", "sp")
ENG2SEM = {"pe": "pe", "dve": "vec", "act": "act", "pool": "pool", "sp": None}
FORCE_F32_STREAM = False
import os
DBG_WBF = bool(int(os.environ.get('KDBG', '0')))
CPOOL_VALS = [1e-5, 1.0, -0.5, -1.0, 1.0 / 1024, 1.0 / 256, 0.0625, 0.5,
              1e-5 * 0.0625 * 0.0625]
DMA_SEMS = {"dxa", "dxb", "dcos", "dsin", "dcl", "dsl", "dyld", "dsmall", "dwa", "dwm", "dout",
            "dbp1", "dbp2", "dbp3", "dbp4", "dbq", "dstga", "dstgb", "dstgb2",
            "dstgc", "dstgd", "dwmb", "dwmc",
            "ds0", "ds1", "ds2", "ds3", "ds4", "ds5"}


class Res:
    __slots__ = ("wr", "rd", "name")

    def __init__(self, name=""):
        self.wr = None
        self.rd = []
        self.name = name


class Bld:
    def __init__(self):
        self.ops = {e: [] for e in ENGS}
        self.cnt = {}
        self.waited = {e: {} for e in ENGS}

    def ms(self, sem):
        return (sem, self.cnt.get(sem, 0))

    def raw(self, eng, fn, waits=(), inc=None, inc_amt=1):
        w = [x for x in waits if x is not None and x[1] > 0]
        tgt = None
        if inc is not None:
            amt = 16 if inc in DMA_SEMS else inc_amt
            self.cnt[inc] = self.cnt.get(inc, 0) + amt
            tgt = self.cnt[inc]
        waited = self.waited[eng]

        def thunk(e, sems, _fn=fn, _w=tuple(w), _inc=inc, _eng=eng, _wd=waited):
            for (sn, val) in _w:
                if _wd.get(sn, 0) < val:
                    e.wait_ge(sems[sn], val)
                    _wd[sn] = val
            inst = _fn(e)
            if _inc is not None:
                inst.then_inc(sems[_inc], 16 if _inc in DMA_SEMS else 1)
        self.ops[eng].append(thunk)
        return (inc, tgt) if inc is not None else None

    def g(self, eng, fn, reads=(), writes=(), waits=(), inc=None):
        if inc is None:
            inc = ENG2SEM[eng]
            assert inc is not None, "sp groups must specify a DMA sem"
        wl = list(waits)
        for r in reads:
            if r.wr is not None:
                wl.append(r.wr)
        for r in writes:
            if r.wr is not None:
                wl.append(r.wr)
            wl.extend(r.rd)
        ms = self.raw(eng, fn, wl, inc=inc)
        for r in writes:
            r.wr = ms
            r.rd = []
        for r in reads:
            r.rd.append(ms)
        return ms

    BARRIER_EXCLUDE = {"ds0", "ds1", "ds2", "ds3", "ds4", "ds5",
                       "dstga", "dstgb", "dstgb2", "dwa", "dwm"}

    def barrier(self, extra=()):
        cur = [self.ms(s) for s in ("pe", "vec", "act", "pool")]
        cur += [self.ms(s) for s in DMA_SEMS if s not in self.BARRIER_EXCLUDE]
        cur += list(extra)
        for eng in ENGS:
            w = [m for m in cur if m is not None and m[1] > 0]
            self.raw(eng, lambda e: e.nop(), w, inc=ENG2SEM[eng])



def _host_consts():
    c = {}
    half = 128
    inv = 1.0 / (10000.0 ** (np.arange(half, dtype=np.float64) / half))
    ang = np.arange(T, dtype=np.float64)[:, None] * inv[None, :]
    cosd, sind = np.cos(ang), np.sin(ang)
    cos = cosd.astype(np.float32)
    sin = sind.astype(np.float32)
    cosl = (cosd - cos.astype(np.float64)).astype(np.float32)
    sinl = (sind - sin.astype(np.float64)).astype(np.float32)
    c["cos4"] = np.ascontiguousarray(np.tile(cos, (1, H)))
    c["sin4"] = np.ascontiguousarray(np.tile(sin, (1, H)))
    c["cos4l"] = np.ascontiguousarray(np.tile(cosl, (1, H)))
    c["sin4l"] = np.ascontiguousarray(np.tile(sinl, (1, H)))
    ut = np.triu(np.ones((128, 128), dtype=np.float32))
    c["ut_fs"] = ut * np.float32(0.0625)
    c["ut_b"] = ut.astype(bfloat16)
    ident = np.eye(128, dtype=np.float32)
    c["id_f"] = ident
    c["id_b"] = ident.astype(bfloat16)
    c["ones_row_f"] = np.ones((1, 128), dtype=np.float32)
    c["ones_row_b"] = np.ones((1, 128), dtype=bfloat16)
    c["ones_col_b"] = np.ones((128, 1), dtype=bfloat16)
    oh = np.zeros((E, 8, 128), dtype=np.float32)
    for e in range(E):
        oh[e, e, :] = 1.0
    c["onehot_b"] = oh.astype(bfloat16)
    c["zrow_b"] = np.zeros((1, D), dtype=bfloat16)
    c["cpool"] = np.tile(np.array(CPOOL_VALS, dtype=np.float32), (128, 1))
    return c


def _prep_inputs(inputs):
    f = np.float32
    ac = np.ascontiguousarray
    d = {}
    d["x_all"] = ac(np.asarray(inputs["L0_attn_hidden_states"], f))
    for nm in ("Wq", "Wk", "Wv", "Wg", "Wo"):
        d[nm.lower()] = ac(np.asarray(inputs["L0_attn_" + nm], f))
    for nm in ("sk_w1", "sv_w1"):
        w = np.asarray(inputs["L0_attn_" + nm], f)
        d[nm] = ac(w.reshape(KB, 128, 16).transpose(1, 0, 2))
    d["sk_w2"] = ac(np.asarray(inputs["L0_attn_sk_w2"], f))
    d["sv_w2"] = ac(np.asarray(inputs["L0_attn_sv_w2"], f))
    d["sk_b2"] = ac(np.asarray(inputs["L0_attn_sk_b2"], f).reshape(1, HM))
    d["sv_b2"] = ac(np.asarray(inputs["L0_attn_sv_b2"], f).reshape(1, HM))
    d["sv_b2c"] = ac(np.asarray(inputs["L0_attn_sv_b2"], f).reshape(H, 64).T)
    d["gnw"] = ac(np.asarray(inputs["L0_attn_gnorm_w"], f).reshape(1, DKH))
    gw = np.asarray(inputs["L0_ffn_gate_w"], f)
    d["gateT"] = ac(gw.T.reshape(KB, 128, E).transpose(1, 0, 2))
    w1 = np.asarray(inputs["L0_ffn_w1"], f)
    w2 = np.asarray(inputs["L0_ffn_w2"], f)
    w3 = np.asarray(inputs["L0_ffn_w3"], f)
    d["w1T"] = ac(w1.transpose(0, 2, 1))
    d["w2T"] = ac(w2.transpose(0, 2, 1))
    d["w3T"] = ac(w3.transpose(0, 2, 1))
    b1 = np.asarray(inputs["L0_ffn_b1"], f)
    d["b1c"] = ac(b1.reshape(E, KB, 128).transpose(0, 2, 1))
    d["b2r"] = ac(np.asarray(inputs["L0_ffn_b2"], f))
    d["b3r"] = ac(np.asarray(inputs["L0_ffn_b3"], f))
    sw1 = np.asarray(inputs["L0_ffn_sw1"], f)
    sw2 = np.asarray(inputs["L0_ffn_sw2"], f)
    sw3 = np.asarray(inputs["L0_ffn_sw3"], f)
    sw1T = sw1.T
    sw2T = sw2.T
    sw3T = sw3.T
    d["sw1Ta"] = ac(sw1T[:, 0:1024]); d["sw1Tb"] = ac(sw1T[:, 1024:2048])
    d["sw2Ta"] = ac(sw2T[:, 0:1024]); d["sw2Tb"] = ac(sw2T[:, 1024:2048])
    d["sw3a"] = ac(sw3T[0:1024, :]);  d["sw3b"] = ac(sw3T[1024:2048, :])
    return d



def build_nc():
    nc = bass.Bass("TRN2", target_bir_lowering=False)
    B = Bld()

    def din(name, shape, dt=F32):
        return nc.dram_tensor(name, list(shape), dt, kind="ExternalInput").ap()

    x_h = din("x", (T, D))
    y_h = nc.dram_tensor("y", [T, D], F32, kind="ExternalOutput").ap()

    wq_h = din("wq", (D, D)); wk_h = din("wk", (D, D)); wv_h = din("wv", (D, D))
    wg_h = din("wg", (D, D)); wo_h = din("wo", (D, D))
    skw1_h = din("sk_w1", (128, KB, 16)); svw1_h = din("sv_w1", (128, KB, 16))
    skw2_h = din("sk_w2", (16, HM)); svw2_h = din("sv_w2", (16, HM))
    skb2_h = din("sk_b2", (1, HM)); svb2_h = din("sv_b2", (1, HM))
    svb2c_h = din("sv_b2c", (64, H))
    gnw_h = din("gnw", (1, DKH))
    gateT_h = din("gateT", (128, KB, E))
    w1T_h = din("w1T", (E, D, D)); w2T_h = din("w2T", (E, D, D)); w3T_h = din("w3T", (E, D, D))
    b1c_h = din("b1c", (E, 128, KB))
    b2r_h = din("b2r", (E, D)); b3r_h = din("b3r", (E, D))
    sw_h = {}
    for nm in ("sw1Ta", "sw1Tb", "sw2Ta", "sw2Tb", "sw3a", "sw3b"):
        sw_h[nm] = din(nm, (D, D))
    cos4_h = din("cos4", (T, 512)); sin4_h = din("sin4", (T, 512))
    cos4l_h = din("cos4l", (T, 512)); sin4l_h = din("sin4l", (T, 512))
    utb_h = din("ut_b", (128, 128), BF16)
    utfs_h = din("ut_fs", (128, 128))
    idf_h = din("id_f", (128, 128)); idb_h = din("id_b", (128, 128), BF16)
    onesrf_h = din("ones_row_f", (1, 128)); onesrb_h = din("ones_row_b", (1, 128), BF16)
    onescb_h = din("ones_col_b", (128, 1), BF16)
    onehot_h = din("onehot_b", (E, 8, 128), BF16)
    zrow_h = din("zrow_b", (1, D), BF16)
    cpool_h = din("cpool_h", (128, len(CPOOL_VALS)))

    if DBG_WBF:
        dbg_wbf = nc.dram_tensor("dbg_wbf", [D, D], BF16, kind="ExternalOutput").ap()
        dbg_slot = nc.dram_tensor("dbg_slot", [128, KB, D], BF16,
                                  kind="ExternalOutput").ap()
        dbg_c0 = {}
        dbg_mwbf = nc.dram_tensor("dbg_mwbf", [D, D], BF16, kind="ExternalOutput").ap()
        dbg_mslot = nc.dram_tensor("dbg_mslot", [128, KB, D], BF16,
                                   kind="ExternalOutput").ap()
        dbg_pt1 = nc.dram_tensor("dbg_pt1", [128, D], F32, kind="ExternalOutput").ap()
        dbg_sw3 = nc.dram_tensor("dbg_sw3", [128, KB, D], BF16, kind="ExternalOutput").ap()
        dbg_sw3c = nc.dram_tensor("dbg_sw3c", [D, D], BF16, kind="ExternalOutput").ap()
        dbg_yep = nc.dram_tensor("dbg_yep", [NEP + 1, 128, D], F32,
                                 kind="ExternalOutput").ap()
        dbg_b2 = nc.dram_tensor("dbg_b2", [NEP, D], BF16, kind="ExternalOutput").ap()
        dbg_b3p = nc.dram_tensor("dbg_b3p", [NEP, 8, D], BF16, kind="ExternalOutput").ap()
        for nm, shp, dt_ in [("xnb", (128, D), BF16), ("qrb", (128, D), BF16),
                             ("krb", (128, D), BF16), ("vb", (128, D), BF16),
                             ("g32", (128, D), F32), ("wkb", (128, HM), BF16),
                             ("wvb", (128, HM), BF16), ("apb", (128, D), BF16),
                             ("q32", (128, D), F32), ("XT", (128, KB, CH), BF16),
                             ("slot0", (128, KB, D), BF16), ("cosS", (128, 512), F32),
                             ("KS", (128, 512), F32), ("zkS", (1, HM), F32)]:
            dbg_c0[nm] = nc.dram_tensor("dbg_" + nm, list(shp), dt_,
                                        kind="ExternalOutput").ap()
    wbf_attn = nc.dram_tensor("wbf_attn", [5, D, D], BF16, kind="Internal").ap()
    wbf_moe = nc.dram_tensor("wbf_moe", [NEP, 3, D, D], BF16, kind="Internal").ap()

    st = contextlib.ExitStack()
    sems = {}

    def sem(name):
        sems[name] = st.enter_context(nc.semaphore(name))

    def sb(name, shape, dt):
        return st.enter_context(nc.sbuf_tensor(name, list(shape), dt))

    with st:
        for s_ in ("pe", "vec", "act", "pool",
                   "dxa", "dxb", "dcos", "dsin", "dcl", "dsl", "dyld", "dsmall", "dwa", "dwm", "dout",
                   "dbp1", "dbp2", "dbp3", "dbp4", "dbq", "dstga", "dstgb", "dstgb2",
                   "dstgc", "dstgd", "dwmb", "dwmc",
                   "ds0", "ds1", "ds2", "ds3", "ds4", "ds5"):
            sem(s_)

        XT = sb("XT", (128, KB, QT), BF16)
        SLOT = [sb(f"slot{i}", (128, KB, D), BF16) for i in range(6)]
        KS = sb("KS", (128, 512), F32); KSb = sb("KSb", (128, 512), BF16)
        VS = [sb(f"VS{h}", (64, DKH), F32) for h in range(H)]
        VSb = [sb(f"VSb{h}", (64, DKH), BF16) for h in range(H)]
        zkS = sb("zkS", (1, HM), F32); zvS = sb("zvS", (1, HM), F32)
        skw1s = sb("skw1s", (128, KB, 16), BF16); svw1s = sb("svw1s", (128, KB, 16), BF16)
        skw2s = sb("skw2s", (16, HM), BF16); svw2s = sb("svw2s", (16, HM), BF16)
        skb2r = sb("skb2r", (1, HM), BF16); svb2r = sb("svb2r", (1, HM), BF16)
        svb2c = sb("svb2c", (64, H), F32)
        gnwbc = sb("gnwbc", (128, DKH), F32)
        gts = sb("gts", (128, KB, E), BF16)
        utb = sb("utb", (128, 128), BF16)
        utfs = sb("utfs", (128, 128), F32)
        idf = sb("idf", (128, 128), F32); idb = sb("idb", (128, 128), BF16)
        onesrf = sb("onesrf", (1, 128), F32); onesrb = sb("onesrb", (1, 128), BF16)
        onescb = sb("onescb", (128, 1), BF16)
        onehot = sb("onehot", (8, E, 128), BF16)
        zrow = sb("zrow", (1, D), BF16)
        cpool = sb("cpool", (128, len(CPOOL_VALS)), F32)
        for _i, _v in enumerate(CPOOL_VALS):
            nc.const_aps.register(F32, _v, cpool[:, _i:_i + 1])
        rowstg = sb("rowstg", (1, D), F32)
        rowbf = sb("rowbf", (1, D), BF16)
        stg0 = sb("stg0", (128, 1, D), F32)
        stg1 = sb("stg1", (128, 1, D), F32)
        stgbf = [sb("stgbf0", (128, 1, D), BF16), sb("stgbf1", (128, 1, D), BF16)]

        with contextlib.ExitStack() as ssA:
            def sa(name, shape, dt):
                return ssA.enter_context(nc.sbuf_tensor(name, list(shape), dt))
            xc32 = [sa("xc32_0", (128, D), F32), sa("xc32_1", (128, D), F32)]
            xnb = sa("xnb", (128, D), BF16)
            q32 = sa("q32", (128, D), F32)
            k32 = sa("k32", (128, D), F32)
            g32 = sa("g32", (128, D), F32)
            o32 = sa("o32", (128, D), F32)
            qrb = sa("qrb", (128, D), BF16)
            krb = sa("krb", (128, D), BF16)
            vb = sa("vb", (128, D), BF16)
            apb = sa("apb", (128, D), BF16)
            qfm = sa("qfm", (128, KB, 128), BF16)
            kfm = sa("kfm", (128, KB, 128), BF16)
            apT = sa("apT", (128, KB, 128), BF16)
            cosS = sa("cosS", (128, 512), F32)
            sinS = sa("sinS", (128, 512), F32)
            tr2 = sa("tr2", (128, 512), F32)
            tr3 = sa("tr3", (128, 512), F32)
            wkb = sa("wkb", (128, HM), BF16)
            wvb = sa("wvb", (128, HM), BF16)
            wvT = [sa(f"wvT{h}", (64, 128), BF16) for h in range(H)]
            aTbk = sa("aTbk", (16, 128), BF16)
            aTbv = sa("aTbv", (16, 128), BF16)
            lzs = sa("lzs", (128, M), F32)
            e_s = sa("e_s", (128, M), F32)
            zkr = sa("zkr", (128, M), F32)
            dnr = sa("dnr", (128, M), F32)
            ptb = sa("ptb", (64, 128), BF16)
            ptb2 = sa("ptb2", (64, 128), BF16)
            Smb = sa("Smb", (128, 128), BF16)
            Smb2 = sa("Smb2", (128, 128), BF16)
            AmT = sa("AmT", (128, 128), BF16)
            AmT2 = sa("AmT2", (128, 128), BF16)
            pcol = [sa(f"pcol{i}", (128, 1), F32) for i in range(8)]
            th32 = sa("th32", (128, DKH), F32)
            gg32 = sa("gg32", (128, DKH), F32)


        with contextlib.ExitStack() as ssB:
            def sbB(name, shape, dt):
                return ssB.enter_context(nc.sbuf_tensor(name, list(shape), dt))
            Y = sbB("Y", (128, NCH, D), F32)
            hfm = sbB("hfm", (128, KB, TIL), BF16)
            hsrc = sbB("hsrc", (128, TIL), F32)
            x3s = sbB("x3s", (128, D), F32)
            obc = sbB("obc", (128, TIL), F32)
            scf = sbB("scf", (128, E), F32)
            sc2 = sbB("sc2", (128, E), F32)
            mks = sbB("mks", (128, E), F32)
            inv_m = sbB("invm", (128, E), F32)
            omg = sbB("omg", (128, NCH, E), F32)
            rhb = sbB("rhb", (128, E), BF16)
            ohb = sbB("ohb", (128, E), BF16)
            oT = sbB("oT", (8, NCH, 128), BF16)
            rT = sbB("rT", (8, NCH, 128), BF16)
            b3pad = sbB("b3pad", (8, D), BF16)
            b2row = sbB("b2row", (1, D), BF16)
            b1cs = sbB("b1cs", (128, KB), F32)
            rcol = [sbB(f"rcol{i}", (128, 1), F32) for i in range(4)]


        with contextlib.ExitStack() as psA:
            def pa_(name, shape, dt):
                return psA.enter_context(nc.psum_tensor(name, list(shape), dt))
            pa0 = pa_("pa0", (128, 512), F32)
            pa1 = pa_("pa1", (128, 512), F32)
            pb2a = pa_("pb2a", (128, 1024), BF16)
            pb2b = pa_("pb2b", (128, 1024), BF16)
            pc0 = pa_("pc0", (128, 512), F32)
            pc1 = pa_("pc1", (128, 512), F32)
            pd0 = pa_("pd0", (128, 512), F32)
            pe0 = pa_("pe0", (128, 512), F32)
        with contextlib.ExitStack() as psB:
            def pb_(name, shape, dt):
                return psB.enter_context(nc.psum_tensor(name, list(shape), dt))
            qa0 = pb_("qa0", (128, 512), F32)
            qa1 = pb_("qa1", (128, 512), F32)
            qb2 = pb_("qb2", (128, 1024), BF16)
            qc0 = pb_("qc0", (128, 512), F32)
            qc1 = pb_("qc1", (128, 512), F32)
            qd0 = pb_("qd0", (128, 512), F32)
            qd1 = pb_("qd1", (128, 512), F32)
            qe0 = pb_("qe0", (128, 512), F32)

        R = {}
        def res(name):
            if name not in R:
                R[name] = Res(name)
            return R[name]
        for n in ("XT", "KS", "KSb", "VS0", "VS1", "VS2", "VS3",
                  "VSb0", "VSb1", "VSb2", "VSb3", "zkS", "zvS",
                  "pa0", "pa1", "pb2a", "pb2b", "pc0", "pc1", "pd0", "pe0",
                  "qa0", "qa1", "qb2", "qc0", "qc1", "qd0", "qd1", "qe0",
                  "xc0", "xc1", "xm32", "xnb", "wstb", "q32", "k32", "g32",
                  "o32", "ob", "qrb", "krb", "vb", "apb", "qfm", "kfm", "apT",
                  "cosS", "sinS", "tr1", "tr2", "tr3", "tr4",
                  "wkb", "wvb", "wvT0", "wvT1", "wvT2", "wvT3", "skc", "svc", "aTbk", "aTbv",
                  "lzs", "e_s", "zkr", "dnr", "ptb", "ptb2", "Smb", "Smb2",
                  "AmT", "AmT2",
                  "th32", "gg32", "gh32", "onw32", "stgA",
                  "Y", "hfm", "hsrc", "x3s", "cmb", "obc", "scf", "sc2",
                  "mks", "invm", "omg", "rhb", "ohb", "oT", "rT",
                  "b3pad", "b2row", "b1cs", "stgB",
                  "smallstg", "rowstg", "rowbf", "wbfa", "wbfm", "stg0", "stg1",
                  "stg2", "stg3", "stgbf0", "stgbf1",
                  "s0", "s1", "s2", "s3", "s4", "s5",
                  "yhbm"):
            res(n)
        for i in range(8):
            res(f"pcol{i}")
        for i in range(4):
            res(f"rcol{i}")

        slotres = [res(f"s{i}") for i in range(6)]
        wbfm_r = [res(f"wbfm{i}") for i in range(NEP)]
        slotsem = ["ds0", "ds1", "ds2", "ds3", "ds4", "ds5"]
        pcolr = [res(f"pcol{i}") for i in range(8)]
        rcolr = [res(f"rcol{i}") for i in range(4)]

        def dma(queue_eng, dst, src, reads=(), writes=(), sem_="dsmall", waits=()):
            return B.g(queue_eng, lambda e, _d=dst, _s=src: e.dma_start(out=_d, in_=_s),
                       reads=reads, writes=writes, waits=waits, inc=sem_)

        dma("sp", utfs[:], utfs_h)
        dma("sp", utb[:], utb_h)
        dma("sp", idf[:], idf_h)
        dma("sp", idb[:], idb_h)
        dma("sp", onesrf[:], onesrf_h)
        dma("sp", onesrb[:], onesrb_h)
        dma("sp", onescb[:], onescb_h)
        dma("sp", onehot[:], onehot_h.rearrange("e k t -> k e t"))
        dma("sp", zrow[:], zrow_h)
        dma("sp", cpool[:], cpool_h)
        dma("sp", svb2c[:], svb2c_h)
        m_consts = B.ms("dsmall")

        v_skw1 = stg0[:, 0, 0:128].rearrange("p (k l) -> p k l", k=KB)
        v_svw1 = stg0[:, 0, 128:256].rearrange("p (k l) -> p k l", k=KB)
        v_skw2 = stg0[0:16, 0, 256:512]
        v_svw2 = stg0[0:16, 0, 512:768]
        v_gate = stg1[:, 0, 0:KB * E].rearrange("p (k e) -> p k e", k=KB)
        v_skb2 = stg1[0:1, 0, 64:64 + HM]
        v_svb2 = stg1[0:1, 0, 320:320 + HM]
        v_gnw = stg1[0:1, 0, 576:576 + DKH]
        dma("sp", v_skw1, skw1_h, writes=[res("stg0")])
        dma("sp", v_svw1, svw1_h, writes=[res("stg0")])
        dma("sp", v_skb2, skb2_h, writes=[res("stg1")])
        dma("sp", v_svb2, svb2_h, writes=[res("stg1")])
        dma("sp", v_skw2, skw2_h, writes=[res("stg0")])
        dma("sp", v_svw2, svw2_h, writes=[res("stg0")])
        dma("sp", v_gate, gateT_h, writes=[res("stg1")])
        dma("sp", v_gnw, gnw_h, writes=[res("stg1")])
        m_allsmall = B.ms("dsmall")
        B.g("pool", lambda e: e.tensor_copy(out=skw1s[:], in_=v_skw1),
            reads=[res("stg0")], waits=[m_allsmall])
        B.g("pool", lambda e: e.tensor_copy(out=svw1s[:], in_=v_svw1),
            reads=[res("stg0")], waits=[m_allsmall])
        B.g("act", lambda e: e.copy(out=skb2r[:], in_=v_skb2),
            reads=[res("stg1")], waits=[m_allsmall])
        B.g("act", lambda e: e.copy(out=svb2r[:], in_=v_svb2),
            reads=[res("stg1")], waits=[m_allsmall])
        B.g("pool", lambda e: e.tensor_copy(out=skw2s[:], in_=v_skw2),
            reads=[res("stg0")], waits=[m_allsmall])
        B.g("pool", lambda e: e.tensor_copy(out=svw2s[:], in_=v_svw2),
            reads=[res("stg0")], waits=[m_allsmall])
        B.g("pool", lambda e: e.tensor_copy(out=gts[:], in_=v_gate),
            reads=[res("stg1")], waits=[m_allsmall])

        def gnw_mm(e):
            return e.matmul(pe0[:, 0:DKH], onesrf[:], v_gnw,
                            start=True, stop=True)
        B.g("pe", gnw_mm, reads=[res("stg1")], writes=[res("pe0")],
            waits=[m_allsmall])
        B.g("dve", lambda e: e.tensor_copy(out=gnwbc[:], in_=pe0[:, 0:DKH]),
            reads=[res("pe0")])

        B.g("dve", lambda e: e.memset(KS[:], 0.0), writes=[res("KS")])
        B.g("dve", lambda e: e.memset(KSb[:], 0.0), writes=[res("KSb")])
        for h in range(H):
            B.g("dve", lambda e, _h=h: e.memset(VS[_h][:], 0.0),
                writes=[res(f"VS{h}")])
            B.g("dve", lambda e, _h=h: e.memset(VSb[_h][:], 0.0),
                writes=[res(f"VSb{h}")])
        B.g("dve", lambda e: e.memset(zkS[:], 0.0), writes=[res("zkS")])
        B.g("dve", lambda e: e.memset(zvS[:], 0.0), writes=[res("zvS")])
        B.barrier(extra=[B.ms("dsmall")])

        AW_SRC = [wq_h, wk_h, wv_h, wg_h, wo_h]

        def load_weight_matrix(slot_t, slot_r, slot_sem, src_f32, src_bf,
                               first, wbsem, stg_t, stg_r, wb_r, stg_sem):
            if first:
                nstg = len(stg_t)
                qmap = ("sp", "pool")
                wqmap = ("pool",)
                wsems = (wbsem,)
                for kb in range(KB):
                    st_t, st_r, st_s = (stg_t[kb % nstg], stg_r[kb % nstg],
                                        stg_sem[kb % nstg])
                    rows = src_f32[kb * 128:(kb + 1) * 128, :]
                    B.g(qmap[kb % nstg],
                        lambda e, _d=st_t[:, 0, :], _s=rows:
                        e.dma_start(out=_d, in_=_s),
                        writes=[st_r], inc=st_s)
                    B.g("pool", lambda e, _kb=kb, _sl=slot_t, _st=st_t:
                        e.tensor_copy(out=_sl[:, _kb, :], in_=_st[:, 0, :]),
                        reads=[st_r], writes=[slot_r])
                    B.g("pool",
                        lambda e, _kb=kb, _sl=slot_t, _wb=src_bf:
                        e.dma_start(out=_wb[_kb * 128:(_kb + 1) * 128, :],
                                    in_=_sl[:, _kb, :]),
                        reads=[slot_r], writes=[wb_r],
                        inc=wsems[kb % len(wsems)])
            else:
                for pr in range(4):
                    rows = src_bf[pr * 256:(pr + 1) * 256, :]
                    B.g("sp",
                        lambda e, _sl=slot_t, _pr=pr, _s=rows.rearrange(
                        "(k p) n -> p k n", p=128): e.dma_start(
                            out=_sl[:, _pr * 2:(_pr + 1) * 2, :], in_=_s),
                        reads=[wb_r], writes=[slot_r], inc=slot_sem)

        def precast_matrix(src_f32, wb_dst, wb_r, pi):
            for kb in range(KB):
                par = (pi * KB + kb) % 2
                st_t, st_r = (stg0, res("stg0")) if par == 0 else (stg1, res("stg1"))
                sb_t, sb_r = stgbf[par], res(f"stgbf{par}")
                rows = src_f32[kb * 128:(kb + 1) * 128, :]
                B.g("pool", lambda e, _d=st_t[:, 0, :], _s=rows:
                    e.dma_start(out=_d, in_=_s),
                    writes=[st_r], inc=("dstga" if par == 0 else "dstgb"))
                B.g("pool", lambda e, _st=st_t, _sb=sb_t:
                    e.tensor_copy(out=_sb[:, 0, :], in_=_st[:, 0, :]),
                    reads=[st_r], writes=[sb_r])
                B.g("pool", lambda e, _kb=kb, _sb=sb_t, _wb=wb_dst:
                    e.dma_start(out=_wb[_kb * 128:(_kb + 1) * 128, :],
                                in_=_sb[:, 0, :]),
                    reads=[sb_r], writes=[wb_r], inc="dwm")

        CORE_BANK = [res("pc0"), res("pc1")]
        CORE_T = [pc0, pc1]

        def phase_a(p):
            first = (p == 0)
            if first:
                for wi in range(5):
                    load_weight_matrix(SLOT[wi], slotres[wi], slotsem[wi],
                                       AW_SRC[wi], wbf_attn[wi],
                                       True, "dwa", [stg0, stg1],
                                       [res("stg0"), res("stg1")],
                                       res("wbfa"), ["dstga", "dstgb"])
            prep_p1(p, 0)
            prep_sk(0)
            prep_qk(p, 0)
            prep_vg(0)
            for c in range(NCH):
                head_pair(c, 0, 1)
                if c + 1 < NCH:
                    prep_p1(p, c + 1)
                head_pair(c, 2, 3)
                chunk_states(c)
                chunk_wo(p, c)
                if c + 1 < NCH:
                    prep_sk(c + 1)
                    prep_qk(p, c + 1)
                    prep_vg(c + 1)

        def prep_p1(p, c):
            toff = p * QT + c * CH
            par = c % 2
            xcr = res(f"xc{par}")
            xct = xc32[par]
            dsem = "dxa" if par == 0 else "dxb"
            B.g("sp", lambda e, _d=xct[:], _s=x_h[toff:toff + CH, :]:
                e.dma_start(out=_d, in_=_s), writes=[xcr], inc=dsem)
            B.g("sp", lambda e, _d=cosS[:], _s=cos4_h[toff:toff + CH, :]:
                e.dma_start(out=_d, in_=_s), writes=[res("cosS")], inc="dcos")
            B.g("sp", lambda e, _d=sinS[:], _s=sin4_h[toff:toff + CH, :]:
                e.dma_start(out=_d, in_=_s), writes=[res("sinS")], inc="dsin")
            B.g("sp", lambda e, _d=tr2[:], _s=cos4l_h[toff:toff + CH, :]:
                e.dma_start(out=_d, in_=_s), writes=[res("tr2")], inc="dcl")
            B.g("sp", lambda e, _d=tr3[:], _s=sin4l_h[toff:toff + CH, :]:
                e.dma_start(out=_d, in_=_s), writes=[res("tr3")], inc="dsl")
            B.g("pool", lambda e: e.tensor_tensor(
                out=cosS[:], in0=cosS[:], in1=tr2[:], op=ALU.add),
                reads=[res("tr2"), res("cosS")], writes=[res("cosS")])
            B.g("pool", lambda e: e.tensor_tensor(
                out=sinS[:], in0=sinS[:], in1=tr3[:], op=ALU.add),
                reads=[res("tr3"), res("sinS")], writes=[res("sinS")])
            B.g("dve", lambda e: e.reduce_sum(out=pcol[1][:], in_=xct[:], axis=AX.X),
                reads=[xcr], writes=[pcolr[1]])
            B.g("dve", lambda e: e.tensor_scalar(
                out=pcol[2][:], in0=pcol[1][:], scalar1=-1.0 / D, scalar2=None,
                op0=ALU.mult), reads=[pcolr[1]], writes=[pcolr[2]])
            B.g("dve", lambda e: e.tensor_scalar(
                out=q32[:], in0=xct[:], scalar1=pcol[2][:], scalar2=None,
                op0=ALU.add), reads=[xcr, pcolr[2]], writes=[res("q32")])
            B.g("act", lambda e: e.activation(
                out=xnb[:], in_=q32[:], func=AF.Square, accum_out=pcol[3][:]),
                reads=[res("q32")], writes=[res("xnb"), pcolr[3]])
            B.g("act", lambda e: e.activation(
                out=pcol[4][:], in_=pcol[3][:], func=AF.Ln, scale=1.0 / D,
                bias=LN_EPS), reads=[pcolr[3]], writes=[pcolr[4]])
            B.g("act", lambda e: e.activation(
                out=pcol[4][:], in_=pcol[4][:], func=AF.Exp, scale=-0.5),
                reads=[pcolr[4]], writes=[pcolr[4]])
            B.g("dve", lambda e: e.tensor_scalar(
                out=xnb[:], in0=q32[:], scalar1=pcol[4][:], scalar2=None,
                op0=ALU.mult), reads=[res("q32"), pcolr[4]], writes=[res("xnb")])
            def xnT(e):
                last = None
                for kb in range(KB):
                    last = e.transpose(pb2a[:, kb * 128:(kb + 1) * 128],
                                       xnb[:, kb * 128:(kb + 1) * 128], idb[:])
                return last
            B.g("pe", xnT, reads=[res("xnb")], writes=[res("pb2a")],
                waits=[m_consts])
            B.g("act", lambda e, _c=c: e.copy(
                out=XT[:, :, _c * CH:(_c + 1) * CH],
                in_=pb2a[:].rearrange("p (k t) -> p k t", k=KB)),
                reads=[res("pb2a")], writes=[res("XT")])

        def prep_sk(c):
            def aT_k(e):
                last = None
                for kb in range(KB):
                    last = e.matmul(pe0[0:16, 0:128], skw1s[:, kb, :],
                                    XT[:, kb, c * CH:(c + 1) * CH],
                                    start=(kb == 0), stop=(kb == KB - 1))
                return last
            B.g("pe", aT_k, reads=[res("XT")], writes=[res("pe0")])
            B.g("dve", lambda e: e.tensor_copy(out=aTbk[:], in_=pe0[0:16, 0:128]),
                reads=[res("pe0")], writes=[res("aTbk")])

            def aT_v(e):
                last = None
                for kb in range(KB):
                    last = e.matmul(pe0[0:16, 128:256], svw1s[:, kb, :],
                                    XT[:, kb, c * CH:(c + 1) * CH],
                                    start=(kb == 0), stop=(kb == KB - 1))
                return last
            B.g("pe", aT_v, reads=[res("XT")], writes=[res("pe0")])
            B.g("dve", lambda e: e.tensor_copy(out=aTbv[:], in_=pe0[0:16, 128:256]),
                reads=[res("pe0")], writes=[res("aTbv")])

            def sk_mm(e):
                e.matmul(pc0[:, 0:HM], aTbk[:], skw2s[:], start=True, stop=False)
                return e.matmul(pc0[:, 0:HM], onesrb[:], skb2r[:],
                                start=False, stop=True)
            B.g("pe", sk_mm, reads=[res("aTbk")], writes=[res("pc0")])
            B.g("dve", lambda e: e.tensor_scalar(
                out=k32[:, 0:HM], in0=pc0[:, 0:HM], scalar1=CLAMP, scalar2=-CLAMP,
                op0=ALU.min, op1=ALU.max), reads=[res("pc0")], writes=[res("k32")])
            B.g("act", lambda e: e.activation(out=wkb[:], in_=k32[:, 0:HM],
                                              func=AF.Exp),
                reads=[res("k32")], writes=[res("wkb")])

            def sv_mm(e):
                e.matmul(pc1[:, 0:HM], aTbv[:], svw2s[:], start=True, stop=False)
                return e.matmul(pc1[:, 0:HM], onesrb[:], svb2r[:],
                                start=False, stop=True)
            B.g("pe", sv_mm, reads=[res("aTbv")], writes=[res("pc1")])
            B.g("dve", lambda e: e.tensor_scalar(
                out=k32[:, HM:2 * HM], in0=pc1[:, 0:HM], scalar1=CLAMP,
                scalar2=-CLAMP, op0=ALU.min, op1=ALU.max),
                reads=[res("pc1")], writes=[res("k32")])
            B.g("act", lambda e: e.activation(out=wvb[:], in_=k32[:, HM:2 * HM],
                                              func=AF.Exp),
                reads=[res("k32")], writes=[res("wvb")])

            def svT_mm(e):
                last = None
                for h in range(H):
                    last = e.matmul(pd0[0:64, h * 128:(h + 1) * 128],
                                    svw2s[:, h * 64:(h + 1) * 64], aTbv[:],
                                    start=True, stop=True)
                return last
            B.g("pe", svT_mm, reads=[res("aTbv")], writes=[res("pd0")])
            for h in range(H):
                B.g("dve", lambda e, _h=h: e.tensor_scalar(
                    out=q32[0:64, _h * 128:(_h + 1) * 128],
                    in0=pd0[0:64, _h * 128:(_h + 1) * 128],
                    scalar1=svb2c[:, _h:_h + 1], scalar2=CLAMP,
                    op0=ALU.add, op1=ALU.min),
                    reads=[res("pd0")], writes=[res("q32")])
            B.g("dve", lambda e: e.tensor_scalar(
                out=q32[0:64, 0:512], in0=q32[0:64, 0:512], scalar1=-CLAMP,
                scalar2=None, op0=ALU.max), reads=[res("q32")], writes=[res("q32")])
            for h in range(H):
                B.g("act", lambda e, _h=h: e.activation(
                    out=wvT[_h][:], in_=q32[0:64, _h * 128:(_h + 1) * 128],
                    func=AF.Exp), reads=[res("q32")], writes=[res(f"wvT{h}")])

        def _proj(c, name, dst32, dstres, tobf=None):
            wslots = {"q": (SLOT[0], slotres[0]), "k": (SLOT[1], slotres[1]),
                      "v": (SLOT[2], slotres[2]), "g": (SLOT[3], slotres[3])}
            slott, slotr = wslots[name]
            for half in range(2):
                bank = res("pa0") if half == 0 else res("pa1")
                bankt = pa0 if half == 0 else pa1

                def mmf(e, _sl=slott, _h=half, _bk=bankt, _c=c):
                    last = None
                    for kb in range(KB):
                        last = e.matmul(_bk[:], XT[:, kb, _c * CH:(_c + 1) * CH],
                                        _sl[:, kb, _h * 512:(_h + 1) * 512],
                                        start=(kb == 0), stop=(kb == KB - 1))
                    return last
                B.g("pe", mmf, reads=[res("XT"), slotr], writes=[bank])
                tgt = tobf if tobf is not None else dst32
                B.g("act", lambda e, _h=half, _bk=bankt, _d=tgt:
                    e.copy(out=_d[:, _h * 512:(_h + 1) * 512], in_=_bk[:]),
                    reads=[bank], writes=[dstres])

        def _rope(src32, srcr, dstb, dstr):
            sv = src32[:].rearrange("p (h x) -> p h x", h=H)
            dv = dstb[:].rearrange("p (h x) -> p h x", h=H)
            s1 = sv[:, :, 0:128]
            s2 = sv[:, :, 128:256]
            d1 = dv[:, :, 0:128]
            d2 = dv[:, :, 128:256]
            cosv = cosS[:].rearrange("p (h d) -> p h d", h=H)
            sinv = sinS[:].rearrange("p (h d) -> p h d", h=H)
            t2v = tr2[:].rearrange("p (h d) -> p h d", h=H)
            t3v = tr3[:].rearrange("p (h d) -> p h d", h=H)
            B.g("pool", lambda e: e.tensor_tensor(
                out=t2v, in0=s1, in1=cosv, op=ALU.mult),
                reads=[srcr, res("cosS")], writes=[res("tr2")])
            B.g("pool", lambda e: e.tensor_tensor(
                out=t3v, in0=s2, in1=sinv, op=ALU.mult),
                reads=[srcr, res("sinS")], writes=[res("tr3")])
            B.g("pool", lambda e: e.tensor_tensor(
                out=d1, in0=t2v, in1=t3v, op=ALU.subtract),
                reads=[res("tr2"), res("tr3")], writes=[dstr])
            B.g("pool", lambda e: e.tensor_tensor(
                out=t2v, in0=s2, in1=cosv, op=ALU.mult),
                reads=[srcr, res("cosS")], writes=[res("tr2")])
            B.g("pool", lambda e: e.tensor_tensor(
                out=t3v, in0=s1, in1=sinv, op=ALU.mult),
                reads=[srcr, res("sinS")], writes=[res("tr3")])
            B.g("pool", lambda e: e.tensor_tensor(
                out=d2, in0=t2v, in1=t3v, op=ALU.add),
                reads=[res("tr2"), res("tr3")], writes=[dstr])

        def prep_qk(p, c):
            _proj(c, "q", q32, res("q32"))
            _proj(c, "k", k32, res("k32"))
            _rope(q32, res("q32"), qrb, res("qrb"))
            _rope(k32, res("k32"), krb, res("krb"))

            def tq(e):
                last = None
                for kb in range(KB):
                    last = e.transpose(pb2b[:, kb * 128:(kb + 1) * 128],
                                       qrb[:, kb * 128:(kb + 1) * 128], idb[:])
                return last
            B.g("pe", tq, reads=[res("qrb")], writes=[res("pb2b")])
            B.g("dve", lambda e: e.tensor_copy(
                out=qfm[:].rearrange("p k t -> p (k t)"), in_=pb2b[:]),
                reads=[res("pb2b")], writes=[res("qfm")])

            def tk(e):
                last = None
                for kb in range(KB):
                    last = e.transpose(pb2a[:, kb * 128:(kb + 1) * 128],
                                       krb[:, kb * 128:(kb + 1) * 128], idb[:])
                return last
            B.g("pe", tk, reads=[res("krb")], writes=[res("pb2a")])
            B.g("dve", lambda e: e.tensor_copy(
                out=kfm[:].rearrange("p k t -> p (k t)"), in_=pb2a[:]),
                reads=[res("pb2a")], writes=[res("kfm")])

        def prep_vg(c):
            _proj(c, "v", None, res("vb"), tobf=vb)
            _proj(c, "g", g32, res("g32"))

        def head_pair(c, a, b):
            ctx = {}
            for x, (bk, bkt, smb, smr, ptt, ptr, amt, amr, zc, zr, mc, mr,
                    psl) in (
                (a, (res("pc0"), pc0, Smb, res("Smb"), ptb, res("ptb"),
                     AmT, res("AmT"), pcol[0], pcolr[0], pcol[5], pcolr[5],
                     (0, 64))),
                (b, (res("pc1"), pc1, Smb2, res("Smb2"), ptb2, res("ptb2"),
                     AmT2, res("AmT2"), pcol[6], pcolr[6], pcol[7], pcolr[7],
                     (64, 128))),
            ):
                ctx[x] = dict(bank=bk, bkt=bkt, smb=smb, smr=smr, ptt=ptt,
                              ptr=ptr, amt=amt, amr=amr, zc=zc, zr=zr, mc=mc,
                              mr=mr, psl=psl)

            def smm(x):
                t = ctx[x]

                def f(e, _h=x, _b=t["bkt"]):
                    e.matmul(_b[:, 0:128], kfm[:, 2 * _h, :], qfm[:, 2 * _h, :],
                             start=True, stop=False)
                    return e.matmul(_b[:, 0:128], kfm[:, 2 * _h + 1, :],
                                    qfm[:, 2 * _h + 1, :], start=False, stop=True)
                B.g("pe", f, reads=[res("kfm"), res("qfm")], writes=[t["bank"]])

            def smask(x):
                t = ctx[x]
                B.g("dve", lambda e, _b=t["bkt"], _s=t["smb"]: e.tensor_tensor(
                    out=_s[:], in0=_b[:, 0:128], in1=utfs[:], op=ALU.mult),
                    reads=[t["bank"]], writes=[t["smr"]])

            def lmm(x):
                t = ctx[x]

                def f(e, _h=x, _b=t["bkt"], _s=t["smb"]):
                    e.matmul(_b[:, 128:192], _s[:], wkb[:, _h * 64:(_h + 1) * 64],
                             start=True, stop=False)
                    e.matmul(_b[:, 128:192], qfm[:, 2 * _h, :],
                             KSb[:, (2 * _h) * 64:(2 * _h + 1) * 64],
                             start=False, stop=False)
                    return e.matmul(_b[:, 128:192], qfm[:, 2 * _h + 1, :],
                                    KSb[:, (2 * _h + 1) * 64:(2 * _h + 2) * 64],
                                    start=False, stop=True)
                B.g("pe", f, reads=[t["smr"], res("qfm"), res("wkb"),
                                    res("KSb")], writes=[t["bank"]])

            def zmm(x):
                t = ctx[x]

                def f(e, _h=x, _b=t["bkt"]):
                    e.matmul(_b[:, 192:256], utb[:], wkb[:, _h * 64:(_h + 1) * 64],
                             start=True, stop=False)
                    e.matmul(_b[:, 192:256], onesrf[:],
                             zkS[0:1, _h * 64:(_h + 1) * 64], start=False, stop=True)
                    e.matmul(_b[:, 256:320], utb[:], wvb[:, _h * 64:(_h + 1) * 64],
                             start=True, stop=False)
                    return e.matmul(_b[:, 256:320], onesrf[:],
                                    zvS[0:1, _h * 64:(_h + 1) * 64],
                                    start=False, stop=True)
                B.g("pe", f, reads=[res("wkb"), res("wvb"), res("zkS"),
                                    res("zvS")], writes=[t["bank"]])

            def soft(x):
                t = ctx[x]
                bkt_ = t["bkt"]
                B.g("dve", lambda e: e.reciprocal(out=zkr[:], in_=bkt_[:, 192:256]),
                    reads=[t["bank"]], writes=[res("zkr")])
                B.g("dve", lambda e: e.tensor_tensor(
                    out=lzs[:], in0=bkt_[:, 128:192], in1=zkr[:], op=ALU.mult),
                    reads=[t["bank"], res("zkr")], writes=[res("lzs")])
                B.g("dve", lambda e, _m=t["mc"]: e.reduce_max(
                    out=_m[:], in_=lzs[:], axis=AX.X, negate=True),
                    reads=[res("lzs")], writes=[t["mr"]])
                B.g("act", lambda e, _m=t["mc"], _z=t["zc"]: e.activation(
                    out=e_s[:], in_=lzs[:], func=AF.Exp, bias=_m[:],
                    scale=1.0, accum_out=_z[:]),
                    reads=[res("lzs"), t["mr"]], writes=[res("e_s"), t["zr"]])
                B.g("act", lambda e, _z=t["zc"]: e.activation(
                    out=dnr[:], in_=bkt_[:, 256:320], func=AF.Copy,
                    scale=_z[:]), reads=[t["bank"], t["zr"]], writes=[res("dnr")])
                B.g("dve", lambda e: e.reciprocal(out=dnr[:], in_=dnr[:]),
                    reads=[res("dnr")], writes=[res("dnr")])
                lo, hi = t["psl"]
                B.g("dve", lambda e, _l=lo, _h=hi: e.tensor_tensor(
                    out=xnb[:, _l:_h], in0=e_s[:], in1=dnr[:], op=ALU.mult),
                    reads=[res("e_s"), res("dnr")], writes=[res("xnb")])

            def ptT(x):
                t = ctx[x]
                lo, hi = t["psl"]
                B.g("pe", lambda e, _l=lo, _h=hi: e.transpose(
                    pb2b[0:64, _l * 2:_l * 2 + 128], xnb[:, _l:_h], idb[:]),
                    reads=[res("xnb")], writes=[res("pb2b")])
                B.g("dve", lambda e, _l=lo, _p=t["ptt"]: e.tensor_copy(
                    out=_p[:], in_=pb2b[0:64, _l * 2:_l * 2 + 128]),
                    reads=[res("pb2b")], writes=[t["ptr"]])

            def amm(x):
                t = ctx[x]
                B.g("pe", lambda e, _h=x, _b=t["bkt"], _p=t["ptt"]: e.matmul(
                    _b[:, 320:448], wvT[_h][:], _p[:], start=True, stop=True),
                    reads=[res(f"wvT{x}"), t["ptr"]], writes=[t["bank"]])

            def amask(x):
                t = ctx[x]
                B.g("dve", lambda e, _b=t["bkt"], _a=t["amt"]: e.tensor_tensor(
                    out=_a[:], in0=_b[:, 320:448], in1=utfs[:], op=ALU.mult),
                    reads=[t["bank"]], writes=[t["amr"]])

            def omm(x):
                t = ctx[x]

                def f(e, _h=x, _b=t["bkt"], _a=t["amt"], _p=t["ptt"]):
                    e.matmul(_b[:, 0:256], _a[:], vb[:, _h * 256:(_h + 1) * 256],
                             start=True, stop=False)
                    return e.matmul(_b[:, 0:256], _p[:], VSb[_h][:],
                                    start=False, stop=True)
                B.g("pe", f, reads=[t["amr"], res("vb"), t["ptr"],
                                    res(f"VSb{x}")], writes=[t["bank"]])

            def oevac(x):
                t = ctx[x]
                B.g("act", lambda e, _h=x, _b=t["bkt"]: e.copy(
                    out=o32[:, _h * 256:(_h + 1) * 256], in_=_b[:, 0:256]),
                    reads=[t["bank"]], writes=[res("o32")])

            def tail(x):
                t = ctx[x]
                mc, mr = t["mc"], t["mr"]
                B.g("act", lambda e, _h=x, _m=mc: e.activation(
                    out=xnb[:, 256:512], in_=o32[:, _h * 256:(_h + 1) * 256],
                    func=AF.Square, accum_out=_m[:]),
                    reads=[res("o32")], writes=[res("xnb"), mr])
                B.g("act", lambda e, _m=mc: e.activation(
                    out=_m[:], in_=_m[:], func=AF.Ln, scale=1.0 / DKH,
                    bias=GN_EPS * ATT_SCALE * ATT_SCALE), reads=[mr], writes=[mr])
                B.g("act", lambda e, _m=mc: e.activation(
                    out=_m[:], in_=_m[:], func=AF.Exp, scale=-0.5),
                    reads=[mr], writes=[mr])
                B.g("act", lambda e, _h=x: e.activation(
                    out=th32[:], in_=g32[:, _h * 256:(_h + 1) * 256], func=AF.Exp,
                    scale=-1.0), reads=[res("g32")], writes=[res("th32")])
                B.g("dve", lambda e, _h=x, _m=mc: e.scalar_tensor_tensor(
                    out=gg32[:], in0=o32[:, _h * 256:(_h + 1) * 256],
                    scalar=_m[:], in1=gnwbc[:], op0=ALU.mult, op1=ALU.mult),
                    reads=[res("o32"), mr], writes=[res("gg32")])
                B.g("dve", lambda e, _h=x: e.tensor_tensor(
                    out=gg32[:], in0=gg32[:],
                    in1=g32[:, _h * 256:(_h + 1) * 256], op=ALU.mult),
                    reads=[res("gg32"), res("g32")], writes=[res("gg32")])
                B.g("dve", lambda e: e.tensor_scalar(
                    out=th32[:], in0=th32[:], scalar1=1.0, scalar2=None,
                    op0=ALU.add), reads=[res("th32")], writes=[res("th32")])
                B.g("dve", lambda e: e.reciprocal(out=th32[:], in_=th32[:]),
                    reads=[res("th32")], writes=[res("th32")])
                B.g("dve", lambda e, _h=x: e.tensor_tensor(
                    out=apb[:, _h * 256:(_h + 1) * 256], in0=gg32[:], in1=th32[:],
                    op=ALU.mult),
                    reads=[res("gg32"), res("th32")], writes=[res("apb")])

            smm(a); smm(b)
            smask(a); smask(b)
            lmm(a); zmm(a); lmm(b); zmm(b)
            soft(a)
            ptT(a); amm(a); amask(a)
            soft(b)
            ptT(b); amm(b)
            omm(a); amask(b); omm(b)
            oevac(a); oevac(b)
            tail(a); tail(b)

        def chunk_states(c):
            def ksd(e):
                last = None
                for blk in range(KB):
                    hh = blk // 2
                    last = e.matmul(pd0[:, blk * 64:(blk + 1) * 64],
                                    krb[:, blk * 128:(blk + 1) * 128],
                                    wkb[:, hh * 64:(hh + 1) * 64],
                                    start=True, stop=True)
                return last
            B.g("pe", ksd, reads=[res("krb"), res("wkb")], writes=[res("pd0")])
            B.g("dve", lambda e: e.tensor_tensor(
                out=KS[:], in0=KS[:], in1=pd0[:], op=ALU.add),
                reads=[res("pd0"), res("KS")], writes=[res("KS")])
            B.g("dve", lambda e: e.tensor_scalar(
                out=KSb[:], in0=KS[:], scalar1=ATT_SCALE, scalar2=None,
                op0=ALU.mult), reads=[res("KS")], writes=[res("KSb")])

            def colsums(e):
                e.matmul(pd0[0:1, 0:HM], onescb[:], wkb[:], start=True, stop=True)
                return e.matmul(pd0[0:1, 256:256 + HM], onescb[:], wvb[:],
                                start=True, stop=True)
            B.g("pe", colsums, reads=[res("wkb"), res("wvb")], writes=[res("pd0")])
            B.g("dve", lambda e: e.tensor_tensor(
                out=zkS[:], in0=zkS[:], in1=pd0[0:1, 0:HM], op=ALU.add),
                reads=[res("pd0"), res("zkS")], writes=[res("zkS")])
            B.g("dve", lambda e: e.tensor_tensor(
                out=zvS[:], in0=zvS[:], in1=pd0[0:1, 256:256 + HM], op=ALU.add),
                reads=[res("pd0"), res("zvS")], writes=[res("zvS")])

            for pair in range(2):
                def vsd(e, _p=pair):
                    e.matmul(pd0[0:64, 0:256], wvb[:, (_p * 2) * 64:(_p * 2 + 1) * 64],
                             vb[:, (_p * 2) * 256:(_p * 2 + 1) * 256],
                             start=True, stop=True)
                    return e.matmul(pd0[0:64, 256:512],
                                    wvb[:, (_p * 2 + 1) * 64:(_p * 2 + 2) * 64],
                                    vb[:, (_p * 2 + 1) * 256:(_p * 2 + 2) * 256],
                                    start=True, stop=True)
                B.g("pe", vsd, reads=[res("wvb"), res("vb")], writes=[res("pd0")])
                for j in range(2):
                    hh = pair * 2 + j
                    B.g("dve", lambda e, _h=hh, _j=j: e.tensor_tensor(
                        out=VS[_h][:], in0=VS[_h][:],
                        in1=pd0[0:64, _j * 256:(_j + 1) * 256], op=ALU.add),
                        reads=[res("pd0"), res(f"VS{hh}")], writes=[res(f"VS{hh}")])
                    B.g("dve", lambda e, _h=hh: e.tensor_scalar(
                        out=VSb[_h][:], in0=VS[_h][:], scalar1=ATT_SCALE,
                        scalar2=None, op0=ALU.mult),
                        reads=[res(f"VS{hh}")], writes=[res(f"VSb{hh}")])

        def chunk_wo(p, c):
            toff = p * QT + c * CH
            par = c % 2
            xcr = res(f"xc{par}")
            xct = xc32[par]

            def tap(e):
                last = None
                for kb in range(KB):
                    last = e.transpose(pb2b[:, kb * 128:(kb + 1) * 128],
                                       apb[:, kb * 128:(kb + 1) * 128], idb[:])
                return last
            B.g("pe", tap, reads=[res("apb")], writes=[res("pb2b")])
            B.g("act", lambda e: e.copy(
                out=apT[:].rearrange("p k t -> p (k t)"), in_=pb2b[:]),
                reads=[res("pb2b")], writes=[res("apT")])

            for half in range(2):
                def wo_mm(e, _h=half):
                    last = None
                    for kb in range(KB):
                        last = e.matmul(pe0[:], apT[:, kb, :],
                                        SLOT[4][:, kb, _h * 512:(_h + 1) * 512],
                                        start=(kb == 0), stop=(kb == KB - 1))
                    return last
                B.g("pe", wo_mm, reads=[res("apT"), slotres[4]], writes=[res("pe0")])
                B.g("dve", lambda e, _h=half: e.tensor_tensor(
                    out=o32[:, _h * 512:(_h + 1) * 512], in0=pe0[:],
                    in1=xct[:, _h * 512:(_h + 1) * 512], op=ALU.add),
                    reads=[res("pe0"), xcr], writes=[res("o32")])
            B.g("pool", lambda e, _t=toff: e.dma_start(
                out=y_h[_t:_t + CH, :], in_=o32[:]),
                reads=[res("o32")], writes=[res("yhbm")], inc="dout")

        MOE_SRC = [(w1T_h[e], w3T_h[e], w2T_h[e]) for e in range(E)]
        MOE_SRC.append((sw_h["sw1Ta"], sw_h["sw2Ta"], sw_h["sw3a"]))
        MOE_SRC.append((sw_h["sw1Tb"], sw_h["sw2Tb"], sw_h["sw3b"]))

        def phase_b(p):
            first = (p == 0) or FORCE_F32_STREAM
            qoff = p * QT
            for scg in range(NCH):
                B.g("pool", lambda e, _s=scg: e.dma_start(
                    out=Y[:, _s, :], in_=y_h[qoff + _s * CH:qoff + (_s + 1) * CH, :]),
                    reads=[res("yhbm")], writes=[res("Y")], inc="dyld")
            for scg in range(NCH):
                routing(p, scg)
            if DBG_WBF and p == 1:
                B.g("pool", lambda e: e.dma_start(out=dbg_yep[0], in_=Y[:, 0, :]),
                    reads=[res("Y")], waits=[B.ms("dbq")], inc="dbq")
            for ep in range(NEP):
                exp_pass(p, ep, first)
                if DBG_WBF and p == 1:
                    B.g("pool", lambda e, _ep=ep: e.dma_start(
                        out=dbg_yep[_ep + 1], in_=Y[:, 0, :]),
                        reads=[res("Y")], waits=[B.ms("dbq")], inc="dbq")
            for scg in range(NCH):
                B.g("pool", lambda e, _s=scg: e.dma_start(
                    out=y_h[qoff + _s * CH:qoff + (_s + 1) * CH, :], in_=Y[:, _s, :]),
                    reads=[res("Y")], writes=[res("yhbm")], inc="dout")

        def routing(p, scg):
            def lg(e):
                last = None
                for kb in range(KB):
                    last = e.matmul(qe0[:, 0:E], XT[:, kb, scg * CH:(scg + 1) * CH],
                                    gts[:, kb, :], start=(kb == 0),
                                    stop=(kb == KB - 1))
                return last
            B.g("pe", lg, reads=[res("XT")], writes=[res("qe0")])
            B.g("dve", lambda e: e.reduce_max(out=rcol[0][:], in_=qe0[:, 0:E],
                                              axis=AX.X),
                reads=[res("qe0")], writes=[rcolr[0]])
            B.g("dve", lambda e: e.tensor_scalar(
                out=rcol[0][:], in0=rcol[0][:], scalar1=-1.0, scalar2=None,
                op0=ALU.mult), reads=[rcolr[0]], writes=[rcolr[0]])
            B.g("act", lambda e: e.activation(
                out=scf[:], in_=qe0[:, 0:E], func=AF.Exp, bias=rcol[0][:],
                accum_out=rcol[1][:]),
                reads=[res("qe0"), rcolr[0]], writes=[res("scf"), rcolr[1]])
            B.g("act", lambda e: e.activation(
                out=rcol[1][:], in_=rcol[1][:], func=AF.Copy),
                reads=[rcolr[1]], writes=[rcolr[1]])
            B.g("dve", lambda e: e.reciprocal(out=rcol[1][:], in_=rcol[1][:]),
                reads=[rcolr[1]], writes=[rcolr[1]])
            B.g("dve", lambda e: e.tensor_scalar(
                out=scf[:], in0=scf[:], scalar1=rcol[1][:], scalar2=None,
                op0=ALU.mult), reads=[res("scf"), rcolr[1]], writes=[res("scf")])
            B.g("dve", lambda e: e.reduce_max(out=rcol[2][:], in_=scf[:], axis=AX.X),
                reads=[res("scf")], writes=[rcolr[2]])
            B.g("dve", lambda e: e.tensor_scalar(
                out=sc2[:], in0=scf[:], scalar1=rcol[2][:], scalar2=None,
                op0=ALU.is_equal), reads=[res("scf"), rcolr[2]], writes=[res("sc2")])
            B.g("dve", lambda e: e.tensor_scalar(
                out=sc2[:], in0=sc2[:], scalar1=-1.0, scalar2=1.0,
                op0=ALU.mult, op1=ALU.add), reads=[res("sc2")], writes=[res("sc2")])
            B.g("dve", lambda e: e.tensor_tensor(
                out=sc2[:], in0=scf[:], in1=sc2[:], op=ALU.mult),
                reads=[res("scf"), res("sc2")], writes=[res("sc2")])
            B.g("dve", lambda e: e.reduce_max(out=rcol[3][:], in_=sc2[:], axis=AX.X),
                reads=[res("sc2")], writes=[rcolr[3]])
            B.g("dve", lambda e: e.tensor_scalar(
                out=mks[:], in0=scf[:], scalar1=rcol[3][:], scalar2=None,
                op0=ALU.is_ge), reads=[res("scf"), rcolr[3]], writes=[res("mks")])
            B.g("dve", lambda e, _s=scg: e.tensor_tensor(
                out=omg[:, _s, :], in0=scf[:], in1=mks[:], op=ALU.mult),
                reads=[res("scf"), res("mks")], writes=[res("omg")])
            B.g("dve", lambda e: e.tensor_scalar(
                out=inv_m[:], in0=mks[:], scalar1=-1.0, scalar2=1.0,
                op0=ALU.mult, op1=ALU.add), reads=[res("mks")], writes=[res("invm")])
            B.g("dve", lambda e, _s=scg: e.tensor_tensor(
                out=scf[:], in0=omg[:, _s, :], in1=inv_m[:], op=ALU.add),
                reads=[res("omg"), res("invm")], writes=[res("scf")])
            B.g("dve", lambda e: e.reciprocal(out=scf[:], in_=scf[:]),
                reads=[res("scf")], writes=[res("scf")])
            B.g("dve", lambda e: e.tensor_copy(out=rhb[:], in_=scf[:]),
                reads=[res("scf")], writes=[res("rhb")])
            B.g("dve", lambda e, _s=scg: e.tensor_copy(out=ohb[:],
                                                       in_=omg[:, _s, :]),
                reads=[res("omg")], writes=[res("ohb")])
            B.g("pe", lambda e: e.transpose(qb2[0:E, 0:128], ohb[:], idb[:]),
                reads=[res("ohb")], writes=[res("qb2")])
            B.g("dve", lambda e, _s=scg: e.tensor_copy(
                out=oT[:, _s, :], in_=qb2[0:E, 0:128]),
                reads=[res("qb2")], writes=[res("oT")])
            B.g("pe", lambda e: e.transpose(qb2[0:E, 0:128], rhb[:], idb[:]),
                reads=[res("rhb")], writes=[res("qb2")])
            B.g("dve", lambda e, _s=scg: e.tensor_copy(
                out=rT[:, _s, :], in_=qb2[0:E, 0:128]),
                reads=[res("qb2")], writes=[res("rT")])

        def exp_pass(p, ep, first):
            shared = (ep >= E)
            base = 3 * (ep % 2)
            cw = [SLOT[base + i] for i in range(3)]
            cr = [slotres[base + i] for i in range(3)]
            csem = [slotsem[base + i] for i in range(3)]
            for i in range(3):
                load_weight_matrix(cw[i], cr[i], csem[i], MOE_SRC[ep][i],
                                   wbf_moe[ep, i], first, "dwm",
                                   [stg0, stg1],
                                   [res("stg0"), res("stg1")],
                                   wbfm_r[ep], ["dstga", "dstgb"])
            if not shared:
                e_idx = ep
                B.g("sp", lambda e: e.dma_start(out=b1cs[:], in_=b1c_h[e_idx]),
                    writes=[res("b1cs")], inc="dbp1")
                B.g("sp", lambda e: e.dma_start(
                    out=rowstg[:], in_=b2r_h[e_idx:e_idx + 1, :]),
                    writes=[res("rowstg")], inc="dbp2")
                B.g("act", lambda e: e.copy(out=b2row[:], in_=rowstg[:]),
                    reads=[res("rowstg")], writes=[res("b2row")])
                B.g("sp", lambda e: e.dma_start(
                    out=rowstg[:], in_=b3r_h[e_idx:e_idx + 1, :]),
                    writes=[res("rowstg")], inc="dbp3")
                B.g("act", lambda e: e.copy(out=rowbf[:], in_=rowstg[:]),
                    reads=[res("rowstg")], writes=[res("rowbf")])
                if e_idx == 0:
                    for r_ in range(1, 8):
                        B.g("pool", lambda e, _r=r_: e.dma_start(
                            out=b3pad[_r:_r + 1, :], in_=zrow[:]),
                            writes=[res("b3pad")], inc="dbp4")
                else:
                    prev = e_idx - 1
                    B.g("pool", lambda e, _p=prev: e.dma_start(
                        out=b3pad[_p:_p + 1, :], in_=zrow[:]),
                        writes=[res("b3pad")], inc="dbp4")
                B.g("pool", lambda e, _e=e_idx: e.dma_start(
                    out=b3pad[_e:_e + 1, :], in_=rowbf[:]),
                    reads=[res("rowbf")], writes=[res("b3pad")], inc="dbp4")
            if DBG_WBF and p == 1 and ep == 0:
                B.g("pool", lambda e, _c=cw[0]: e.dma_start(out=dbg_mslot, in_=_c[:]),
                    reads=[cr[0]], waits=[B.ms("dbq")], inc="dbq")
            if DBG_WBF and p == 1 and ep == 8:
                B.g("pool", lambda e, _c=cw[2]: e.dma_start(out=dbg_sw3, in_=_c[:]),
                    reads=[cr[2]], waits=[B.ms("dbq")], inc="dbq")
            if DBG_WBF and p == 1 and not shared:
                B.g("pool", lambda e, _ep=ep: e.dma_start(out=dbg_b2[_ep], in_=b2row[:]),
                    reads=[res("b2row")], waits=[B.ms("dbq")], inc="dbq")
                B.g("pool", lambda e, _ep=ep: e.dma_start(out=dbg_b3p[_ep], in_=b3pad[:]),
                    reads=[res("b3pad")], waits=[B.ms("dbq")], inc="dbq")
            for t in range(NTIL):
                tile_b(p, ep, t, cw, cr, shared)

        def tile_b(p, ep, t, cw, cr, shared):
            e_idx = ep
            tsl = slice(t * TIL, (t + 1) * TIL)
            if not shared:
                def obmm(e):
                    return e.matmul(qe0[:], onehot[:, e_idx, :],
                                    oT[:, t * NSUB:(t + 1) * NSUB, :].rearrange(
                                        "k a b -> k (a b)"),
                                    start=True, stop=True)
                B.g("pe", obmm, reads=[res("oT")], writes=[res("qe0")])
                B.g("dve", lambda e: e.tensor_copy(out=obc[:], in_=qe0[:]),
                    reads=[res("qe0")], writes=[res("obc")])

            for blk in range(KB):
                bk = [res("qa0"), res("qa1")][blk % 2]
                bkt = [qa0, qa1][blk % 2]
                if not shared:
                    def a1mm(e, _b=blk, _t=bkt):
                        last = None
                        for kb in range(KB):
                            last = e.matmul(_t[:], cw[0][:, kb, _b * 128:(_b + 1) * 128],
                                            XT[:, kb, tsl], start=(kb == 0),
                                            stop=(kb == KB - 1))
                        return last
                    B.g("pe", a1mm, reads=[cr[0], res("XT")], writes=[bk])
                    B.g("dve", lambda e, _t=bkt: e.tensor_tensor(
                        out=hsrc[:], in0=_t[:], in1=obc[:], op=ALU.mult),
                        reads=[bk, res("obc")], writes=[res("hsrc")])
                    B.g("act", lambda e, _b=blk: e.activation(
                        out=hfm[:, _b, :], in_=hsrc[:], func=AF.Silu,
                        bias=b1cs[:, _b:_b + 1], scale=1.0),
                        reads=[res("hsrc"), res("b1cs")], writes=[res("hfm")])
                else:
                    bk2 = [res("qc0"), res("qc1")][blk % 2]
                    bkt2 = [qc0, qc1][blk % 2]

                    def u1mm(e, _b=blk, _t=bkt):
                        last = None
                        for kb in range(KB):
                            last = e.matmul(_t[:], cw[0][:, kb, _b * 128:(_b + 1) * 128],
                                            XT[:, kb, tsl], start=(kb == 0),
                                            stop=(kb == KB - 1))
                        return last
                    B.g("pe", u1mm, reads=[cr[0], res("XT")], writes=[bk])

                    def u2mm(e, _b=blk, _t=bkt2):
                        last = None
                        for kb in range(KB):
                            last = e.matmul(_t[:], cw[1][:, kb, _b * 128:(_b + 1) * 128],
                                            XT[:, kb, tsl], start=(kb == 0),
                                            stop=(kb == KB - 1))
                        return last
                    B.g("pe", u2mm, reads=[cr[1], res("XT")], writes=[bk2])
                    B.g("act", lambda e, _t=bkt: e.activation(
                        out=hsrc[:], in_=_t[:], func=AF.Silu),
                        reads=[bk], writes=[res("hsrc")])
                    B.g("dve", lambda e, _b=blk, _t=bkt2: e.tensor_tensor(
                        out=hfm[:, _b, :], in0=_t[:], in1=hsrc[:], op=ALU.mult),
                        reads=[bk2, res("hsrc")], writes=[res("hfm")])

            for sc in range(NSUB):
                scg = t * NSUB + sc
                ssl = slice(scg * CH, (scg + 1) * CH)
                if not shared:
                    def a3mm(e, _ssl=ssl, _sg=scg):
                        last = None
                        for kb in range(KB):
                            e.matmul(qc0[:], XT[:, kb, _ssl], cw[1][:, kb, 0:512],
                                     start=(kb == 0), stop=False)
                            last = e.matmul(qc1[:], XT[:, kb, _ssl],
                                            cw[1][:, kb, 512:1024],
                                            start=(kb == 0), stop=False)
                        e.matmul(qc0[:], rT[:, _sg, :], b3pad[:, 0:512],
                                 start=False, stop=True)
                        return e.matmul(qc1[:], rT[:, _sg, :], b3pad[:, 512:1024],
                                        start=False, stop=True)
                    B.g("pe", a3mm, reads=[res("XT"), cr[1], res("rT"),
                                           res("b3pad")],
                        writes=[res("qc0"), res("qc1")])
                    B.g("act", lambda e, _s=scg: e.activation(
                        out=x3s[:, 0:512], in_=qc0[:], func=AF.Copy,
                        scale=omg[:, _s, e_idx:e_idx + 1]),
                        reads=[res("qc0"), res("omg")], writes=[res("x3s")])
                    B.g("act", lambda e, _s=scg: e.activation(
                        out=x3s[:, 512:1024], in_=qc1[:], func=AF.Copy,
                        scale=omg[:, _s, e_idx:e_idx + 1]),
                        reads=[res("qc1"), res("omg")], writes=[res("x3s")])
                    def x2mm(e, _sc=sc):
                        last = None
                        for kb in range(KB):
                            e.matmul(qd0[:], hfm[:, kb, _sc * CH:(_sc + 1) * CH],
                                     cw[2][:, kb, 0:512],
                                     start=(kb == 0), stop=False)
                            last = e.matmul(qd1[:],
                                            hfm[:, kb, _sc * CH:(_sc + 1) * CH],
                                            cw[2][:, kb, 512:1024],
                                            start=(kb == 0), stop=False)
                        e.matmul(qd0[:], onesrb[:], b2row[0:1, 0:512],
                                 start=False, stop=True)
                        return e.matmul(qd1[:], onesrb[:], b2row[0:1, 512:1024],
                                        start=False, stop=True)
                    B.g("pe", x2mm, reads=[res("hfm"), cr[2], res("b2row")],
                        writes=[res("qd0"), res("qd1")])
                    for half in range(2):
                        qd = [res("qd0"), res("qd1")][half]
                        qdt = [qd0, qd1][half]
                        B.g("dve", lambda e, _h=half, _t=qdt: e.tensor_tensor(
                            out=x3s[:, _h * 512:(_h + 1) * 512], in0=_t[:],
                            in1=x3s[:, _h * 512:(_h + 1) * 512], op=ALU.mult),
                            reads=[qd], writes=[res("x3s")])
                        B.g("dve", lambda e, _h=half, _s=scg: e.tensor_tensor(
                            out=Y[:, _s, _h * 512:(_h + 1) * 512],
                            in0=Y[:, _s, _h * 512:(_h + 1) * 512],
                            in1=x3s[:, _h * 512:(_h + 1) * 512], op=ALU.add),
                            reads=[res("x3s")], writes=[res("Y")])
                else:
                    def zmm2(e, _sc=sc):
                        last = None
                        for kb in range(KB):
                            e.matmul(qd0[:], hfm[:, kb, _sc * CH:(_sc + 1) * CH],
                                     cw[2][:, kb, 0:512],
                                     start=(kb == 0), stop=(kb == KB - 1))
                            last = e.matmul(qd1[:],
                                            hfm[:, kb, _sc * CH:(_sc + 1) * CH],
                                            cw[2][:, kb, 512:1024],
                                            start=(kb == 0), stop=(kb == KB - 1))
                        return last
                    B.g("pe", zmm2, reads=[res("hfm"), cr[2]],
                        writes=[res("qd0"), res("qd1")])
                    for half in range(2):
                        qd = [res("qd0"), res("qd1")][half]
                        qdt = [qd0, qd1][half]
                        B.g("dve", lambda e, _h=half, _s=scg, _t=qdt: e.tensor_tensor(
                            out=Y[:, _s, _h * 512:(_h + 1) * 512],
                            in0=Y[:, _s, _h * 512:(_h + 1) * 512],
                            in1=_t[:], op=ALU.add),
                            reads=[qd], writes=[res("Y")])

        for p in range(NPASS):
            phase_a(p)
            B.barrier()
            phase_b(p)
            if p + 1 < NPASS:
                for wi in range(5):
                    load_weight_matrix(SLOT[wi], slotres[wi], slotsem[wi],
                                       AW_SRC[wi], wbf_attn[wi],
                                       False, "dwa", None, None,
                                       res("wbfa"), None)
            B.barrier()

        if DBG_WBF:
            B.g("pool", lambda e: e.dma_start(out=dbg_mwbf, in_=wbf_moe[0, 0]),
                reads=[res("wbfm")], waits=[B.ms("dbq")], inc="dbq")
            B.g("pool", lambda e: e.dma_start(out=dbg_sw3c, in_=wbf_moe[8, 2]),
                reads=[res("wbfm")], waits=[B.ms("dbq")], inc="dbq")
            B.g("pool", lambda e: e.dma_start(out=dbg_wbf, in_=wbf_attn[0]),
                reads=[res("wbfa")], waits=[B.ms("dbq")], inc="dbq")
            B.raw("pool", lambda e: e.nop(), [B.ms("dbq")], inc="pool")
        B.raw("sp", lambda e: e.nop(), [B.ms("dout")])

        with nc.Block() as block:
            @block.tensor
            def _(eng):
                for th in B.ops["pe"]:
                    th(eng, sems)

            @block.vector
            def _(eng):
                for th in B.ops["dve"]:
                    th(eng, sems)

            @block.scalar
            def _(eng):
                for th in B.ops["act"]:
                    th(eng, sems)

            @block.gpsimd
            def _(eng):
                for th in B.ops["pool"]:
                    th(eng, sems)

            @block.sync
            def _(eng):
                for th in B.ops["sp"]:
                    th(eng, sems)

    return nc


_NC_CACHE = {}


def kernel(**inputs):
    d = _prep_inputs(inputs)
    c = _host_consts()
    if "nc" not in _NC_CACHE:
        _NC_CACHE["nc"] = build_nc()
    nc = _NC_CACHE["nc"]

    shared = {
        "wq": d["wq"], "wk": d["wk"], "wv": d["wv"], "wg": d["wg"], "wo": d["wo"],
        "sk_w1": d["sk_w1"], "sv_w1": d["sv_w1"],
        "sk_w2": d["sk_w2"], "sv_w2": d["sv_w2"],
        "sk_b2": d["sk_b2"], "sv_b2": d["sv_b2"], "sv_b2c": d["sv_b2c"],
        "gnw": d["gnw"], "gateT": d["gateT"],
        "w1T": d["w1T"], "w2T": d["w2T"], "w3T": d["w3T"],
        "b1c": d["b1c"], "b2r": d["b2r"], "b3r": d["b3r"],
        "sw1Ta": d["sw1Ta"], "sw1Tb": d["sw1Tb"],
        "sw2Ta": d["sw2Ta"], "sw2Tb": d["sw2Tb"],
        "sw3a": d["sw3a"], "sw3b": d["sw3b"],
        "cos4": c["cos4"], "sin4": c["sin4"],
        "cos4l": c["cos4l"], "sin4l": c["sin4l"],
        "ut_fs": c["ut_fs"], "ut_b": c["ut_b"],
        "id_f": c["id_f"], "id_b": c["id_b"],
        "ones_row_f": c["ones_row_f"], "ones_row_b": c["ones_row_b"],
        "ones_col_b": c["ones_col_b"], "onehot_b": c["onehot_b"],
        "zrow_b": c["zrow_b"], "cpool_h": c["cpool"],
    }
    in_maps = []
    for b in range(8):
        m = dict(shared)
        m["x"] = np.ascontiguousarray(d["x_all"][b])
        in_maps.append(m)

    res = run_bass_kernel_spmd(nc, in_maps, core_ids=list(range(8)))
    out = np.stack([r["y"] for r in res.results], axis=0)
    return out.astype(np.float32)


# version 24
# speedup vs baseline: 1.0232x; 1.0232x over previous
import numpy as np
import ml_dtypes
import contextlib

import concourse.bass as bass
import concourse.mybir as mybir
from concourse.bass_utils import run_bass_kernel_spmd

F32 = mybir.dt.float32
BF16 = mybir.dt.bfloat16
AF = mybir.ActivationFunctionType
ALU = mybir.AluOpType
AX = mybir.AxisListType
bfloat16 = ml_dtypes.bfloat16

T, D, H = 4096, 1024, 4
DKH = D // H
M = 64
HM = H * M
E = 8
KB = D // 128
NPASS = 4
QT = T // NPASS
CH = 128
NCH = QT // CH
TIL = 512
NTIL = QT // TIL
NSUB = TIL // CH
ISH = 2048
NEP = E + 2
LN_EPS = 1e-5
GN_EPS = 1e-5
ATT_SCALE = 0.0625
CLAMP = 32.0

ENGS = ("pe", "dve", "act", "pool", "sp")
ENG2SEM = {"pe": "pe", "dve": "vec", "act": "act", "pool": "pool", "sp": None}
FORCE_F32_STREAM = False
import os
DBG_WBF = bool(int(os.environ.get('KDBG', '0')))
CPOOL_VALS = [1e-5, 1.0, -0.5, -1.0, 1.0 / 1024, 1.0 / 256, 0.0625, 0.5,
              1e-5 * 0.0625 * 0.0625]
DMA_SEMS = {"dxa", "dxb", "dcos", "dsin", "dcl", "dsl", "dyld", "dsmall", "dwa", "dwm", "dout",
            "dbp1", "dbp2", "dbp3", "dbp4", "dbq", "dstga", "dstgb", "dstgb2",
            "dstgc", "dstgd", "dwmb", "dwmc",
            "ds0", "ds1", "ds2", "ds3", "ds4", "ds5"}


class Res:
    __slots__ = ("wr", "rd", "name")

    def __init__(self, name=""):
        self.wr = None
        self.rd = []
        self.name = name


class Bld:
    def __init__(self):
        self.ops = {e: [] for e in ENGS}
        self.cnt = {}
        self.waited = {e: {} for e in ENGS}

    def ms(self, sem):
        return (sem, self.cnt.get(sem, 0))

    def raw(self, eng, fn, waits=(), inc=None, inc_amt=1):
        w = [x for x in waits if x is not None and x[1] > 0]
        tgt = None
        if inc is not None:
            amt = 16 if inc in DMA_SEMS else inc_amt
            self.cnt[inc] = self.cnt.get(inc, 0) + amt
            tgt = self.cnt[inc]
        waited = self.waited[eng]

        def thunk(e, sems, _fn=fn, _w=tuple(w), _inc=inc, _eng=eng, _wd=waited):
            for (sn, val) in _w:
                if _wd.get(sn, 0) < val:
                    e.wait_ge(sems[sn], val)
                    _wd[sn] = val
            inst = _fn(e)
            if _inc is not None:
                inst.then_inc(sems[_inc], 16 if _inc in DMA_SEMS else 1)
        self.ops[eng].append(thunk)
        return (inc, tgt) if inc is not None else None

    def g(self, eng, fn, reads=(), writes=(), waits=(), inc=None):
        if inc is None:
            inc = ENG2SEM[eng]
            assert inc is not None, "sp groups must specify a DMA sem"
        wl = list(waits)
        for r in reads:
            if r.wr is not None:
                wl.append(r.wr)
        for r in writes:
            if r.wr is not None:
                wl.append(r.wr)
            wl.extend(r.rd)
        ms = self.raw(eng, fn, wl, inc=inc)
        for r in writes:
            r.wr = ms
            r.rd = []
        for r in reads:
            r.rd.append(ms)
        return ms

    BARRIER_EXCLUDE = {"ds0", "ds1", "ds2", "ds3", "ds4", "ds5",
                       "dstga", "dstgb", "dstgb2", "dwa", "dwm"}

    def barrier(self, extra=()):
        cur = [self.ms(s) for s in ("pe", "vec", "act", "pool")]
        cur += [self.ms(s) for s in DMA_SEMS if s not in self.BARRIER_EXCLUDE]
        cur += list(extra)
        for eng in ENGS:
            w = [m for m in cur if m is not None and m[1] > 0]
            self.raw(eng, lambda e: e.nop(), w, inc=ENG2SEM[eng])



def _host_consts():
    c = {}
    half = 128
    inv = 1.0 / (10000.0 ** (np.arange(half, dtype=np.float64) / half))
    ang = np.arange(T, dtype=np.float64)[:, None] * inv[None, :]
    cosd, sind = np.cos(ang), np.sin(ang)
    cos = cosd.astype(np.float32)
    sin = sind.astype(np.float32)
    cosl = (cosd - cos.astype(np.float64)).astype(np.float32)
    sinl = (sind - sin.astype(np.float64)).astype(np.float32)
    c["cos4"] = np.ascontiguousarray(np.tile(cos, (1, H)))
    c["sin4"] = np.ascontiguousarray(np.tile(sin, (1, H)))
    c["cos4l"] = np.ascontiguousarray(np.tile(cosl, (1, H)))
    c["sin4l"] = np.ascontiguousarray(np.tile(sinl, (1, H)))
    ut = np.triu(np.ones((128, 128), dtype=np.float32))
    c["ut_fs"] = ut * np.float32(0.0625)
    c["ut_b"] = ut.astype(bfloat16)
    ident = np.eye(128, dtype=np.float32)
    c["id_f"] = ident
    c["id_b"] = ident.astype(bfloat16)
    c["ones_row_f"] = np.ones((1, 128), dtype=np.float32)
    c["ones_row_b"] = np.ones((1, 128), dtype=bfloat16)
    c["ones_col_b"] = np.ones((128, 1), dtype=bfloat16)
    oh = np.zeros((E, 8, 128), dtype=np.float32)
    for e in range(E):
        oh[e, e, :] = 1.0
    c["onehot_b"] = oh.astype(bfloat16)
    c["zrow_b"] = np.zeros((1, D), dtype=bfloat16)
    c["cpool"] = np.tile(np.array(CPOOL_VALS, dtype=np.float32), (128, 1))
    return c


def _prep_inputs(inputs):
    f = np.float32
    ac = np.ascontiguousarray
    d = {}
    d["x_all"] = ac(np.asarray(inputs["L0_attn_hidden_states"], f))
    for nm in ("Wq", "Wk", "Wv", "Wg", "Wo"):
        d[nm.lower()] = ac(np.asarray(inputs["L0_attn_" + nm], f))
    for nm in ("sk_w1", "sv_w1"):
        w = np.asarray(inputs["L0_attn_" + nm], f)
        d[nm] = ac(w.reshape(KB, 128, 16).transpose(1, 0, 2))
    d["sk_w2"] = ac(np.asarray(inputs["L0_attn_sk_w2"], f))
    d["sv_w2"] = ac(np.asarray(inputs["L0_attn_sv_w2"], f))
    d["sk_b2"] = ac(np.asarray(inputs["L0_attn_sk_b2"], f).reshape(1, HM))
    d["sv_b2"] = ac(np.asarray(inputs["L0_attn_sv_b2"], f).reshape(1, HM))
    d["sv_b2c"] = ac(np.asarray(inputs["L0_attn_sv_b2"], f).reshape(H, 64).T)
    d["gnw"] = ac(np.asarray(inputs["L0_attn_gnorm_w"], f).reshape(1, DKH))
    gw = np.asarray(inputs["L0_ffn_gate_w"], f)
    d["gateT"] = ac(gw.T.reshape(KB, 128, E).transpose(1, 0, 2))
    w1 = np.asarray(inputs["L0_ffn_w1"], f)
    w2 = np.asarray(inputs["L0_ffn_w2"], f)
    w3 = np.asarray(inputs["L0_ffn_w3"], f)
    d["w1T"] = ac(w1.transpose(0, 2, 1))
    d["w2T"] = ac(w2.transpose(0, 2, 1))
    d["w3T"] = ac(w3.transpose(0, 2, 1))
    b1 = np.asarray(inputs["L0_ffn_b1"], f)
    d["b1c"] = ac(b1.reshape(E, KB, 128).transpose(0, 2, 1))
    d["b2r"] = ac(np.asarray(inputs["L0_ffn_b2"], f))
    d["b3r"] = ac(np.asarray(inputs["L0_ffn_b3"], f))
    sw1 = np.asarray(inputs["L0_ffn_sw1"], f)
    sw2 = np.asarray(inputs["L0_ffn_sw2"], f)
    sw3 = np.asarray(inputs["L0_ffn_sw3"], f)
    sw1T = sw1.T
    sw2T = sw2.T
    sw3T = sw3.T
    d["sw1Ta"] = ac(sw1T[:, 0:1024]); d["sw1Tb"] = ac(sw1T[:, 1024:2048])
    d["sw2Ta"] = ac(sw2T[:, 0:1024]); d["sw2Tb"] = ac(sw2T[:, 1024:2048])
    d["sw3a"] = ac(sw3T[0:1024, :]);  d["sw3b"] = ac(sw3T[1024:2048, :])
    return d



def build_nc():
    nc = bass.Bass("TRN2", target_bir_lowering=False)
    B = Bld()

    def din(name, shape, dt=F32):
        return nc.dram_tensor(name, list(shape), dt, kind="ExternalInput").ap()

    x_h = din("x", (T, D))
    y_h = nc.dram_tensor("y", [T, D], F32, kind="ExternalOutput").ap()

    wq_h = din("wq", (D, D)); wk_h = din("wk", (D, D)); wv_h = din("wv", (D, D))
    wg_h = din("wg", (D, D)); wo_h = din("wo", (D, D))
    skw1_h = din("sk_w1", (128, KB, 16)); svw1_h = din("sv_w1", (128, KB, 16))
    skw2_h = din("sk_w2", (16, HM)); svw2_h = din("sv_w2", (16, HM))
    skb2_h = din("sk_b2", (1, HM)); svb2_h = din("sv_b2", (1, HM))
    svb2c_h = din("sv_b2c", (64, H))
    gnw_h = din("gnw", (1, DKH))
    gateT_h = din("gateT", (128, KB, E))
    w1T_h = din("w1T", (E, D, D)); w2T_h = din("w2T", (E, D, D)); w3T_h = din("w3T", (E, D, D))
    b1c_h = din("b1c", (E, 128, KB))
    b2r_h = din("b2r", (E, D)); b3r_h = din("b3r", (E, D))
    sw_h = {}
    for nm in ("sw1Ta", "sw1Tb", "sw2Ta", "sw2Tb", "sw3a", "sw3b"):
        sw_h[nm] = din(nm, (D, D))
    cos4_h = din("cos4", (T, 512)); sin4_h = din("sin4", (T, 512))
    cos4l_h = din("cos4l", (T, 512)); sin4l_h = din("sin4l", (T, 512))
    utb_h = din("ut_b", (128, 128), BF16)
    utfs_h = din("ut_fs", (128, 128))
    idf_h = din("id_f", (128, 128)); idb_h = din("id_b", (128, 128), BF16)
    onesrf_h = din("ones_row_f", (1, 128)); onesrb_h = din("ones_row_b", (1, 128), BF16)
    onescb_h = din("ones_col_b", (128, 1), BF16)
    onehot_h = din("onehot_b", (E, 8, 128), BF16)
    zrow_h = din("zrow_b", (1, D), BF16)
    cpool_h = din("cpool_h", (128, len(CPOOL_VALS)))

    if DBG_WBF:
        dbg_wbf = nc.dram_tensor("dbg_wbf", [D, D], BF16, kind="ExternalOutput").ap()
        dbg_slot = nc.dram_tensor("dbg_slot", [128, KB, D], BF16,
                                  kind="ExternalOutput").ap()
        dbg_c0 = {}
        dbg_mwbf = nc.dram_tensor("dbg_mwbf", [D, D], BF16, kind="ExternalOutput").ap()
        dbg_mslot = nc.dram_tensor("dbg_mslot", [128, KB, D], BF16,
                                   kind="ExternalOutput").ap()
        dbg_pt1 = nc.dram_tensor("dbg_pt1", [128, D], F32, kind="ExternalOutput").ap()
        dbg_sw3 = nc.dram_tensor("dbg_sw3", [128, KB, D], BF16, kind="ExternalOutput").ap()
        dbg_sw3c = nc.dram_tensor("dbg_sw3c", [D, D], BF16, kind="ExternalOutput").ap()
        dbg_yep = nc.dram_tensor("dbg_yep", [NEP + 1, 128, D], F32,
                                 kind="ExternalOutput").ap()
        dbg_b2 = nc.dram_tensor("dbg_b2", [NEP, D], BF16, kind="ExternalOutput").ap()
        dbg_b3p = nc.dram_tensor("dbg_b3p", [NEP, 8, D], BF16, kind="ExternalOutput").ap()
        for nm, shp, dt_ in [("xnb", (128, D), BF16), ("qrb", (128, D), BF16),
                             ("krb", (128, D), BF16), ("vb", (128, D), BF16),
                             ("g32", (128, D), F32), ("wkb", (128, HM), BF16),
                             ("wvb", (128, HM), BF16), ("apb", (128, D), BF16),
                             ("q32", (128, D), F32), ("XT", (128, KB, CH), BF16),
                             ("slot0", (128, KB, D), BF16), ("cosS", (128, 512), F32),
                             ("KS", (128, 512), F32), ("zkS", (1, HM), F32)]:
            dbg_c0[nm] = nc.dram_tensor("dbg_" + nm, list(shp), dt_,
                                        kind="ExternalOutput").ap()
    wbf_attn = nc.dram_tensor("wbf_attn", [5, D, D], BF16, kind="Internal").ap()
    wbf_moe = nc.dram_tensor("wbf_moe", [NEP, 3, D, D], BF16, kind="Internal").ap()

    st = contextlib.ExitStack()
    sems = {}

    def sem(name):
        sems[name] = st.enter_context(nc.semaphore(name))

    def sb(name, shape, dt):
        return st.enter_context(nc.sbuf_tensor(name, list(shape), dt))

    with st:
        for s_ in ("pe", "vec", "act", "pool",
                   "dxa", "dxb", "dcos", "dsin", "dcl", "dsl", "dyld", "dsmall", "dwa", "dwm", "dout",
                   "dbp1", "dbp2", "dbp3", "dbp4", "dbq", "dstga", "dstgb", "dstgb2",
                   "dstgc", "dstgd", "dwmb", "dwmc",
                   "ds0", "ds1", "ds2", "ds3", "ds4", "ds5"):
            sem(s_)

        XT = sb("XT", (128, KB, QT), BF16)
        SLOT = [sb(f"slot{i}", (128, KB, D), BF16) for i in range(6)]
        KS = sb("KS", (128, 512), F32); KSb = sb("KSb", (128, 512), BF16)
        VS = [sb(f"VS{h}", (64, DKH), F32) for h in range(H)]
        VSb = [sb(f"VSb{h}", (64, DKH), BF16) for h in range(H)]
        zkS = sb("zkS", (1, HM), F32); zvS = sb("zvS", (1, HM), F32)
        skw1s = sb("skw1s", (128, KB, 16), BF16); svw1s = sb("svw1s", (128, KB, 16), BF16)
        skw2s = sb("skw2s", (16, HM), BF16); svw2s = sb("svw2s", (16, HM), BF16)
        skb2r = sb("skb2r", (1, HM), BF16); svb2r = sb("svb2r", (1, HM), BF16)
        svb2c = sb("svb2c", (64, H), F32)
        gnwbc = sb("gnwbc", (128, DKH), F32)
        gts = sb("gts", (128, KB, E), BF16)
        utb = sb("utb", (128, 128), BF16)
        utfs = sb("utfs", (128, 128), F32)
        idf = sb("idf", (128, 128), F32); idb = sb("idb", (128, 128), BF16)
        onesrf = sb("onesrf", (1, 128), F32); onesrb = sb("onesrb", (1, 128), BF16)
        onescb = sb("onescb", (128, 1), BF16)
        onehot = sb("onehot", (8, E, 128), BF16)
        zrow = sb("zrow", (1, D), BF16)
        cpool = sb("cpool", (128, len(CPOOL_VALS)), F32)
        for _i, _v in enumerate(CPOOL_VALS):
            nc.const_aps.register(F32, _v, cpool[:, _i:_i + 1])
        rowstg = sb("rowstg", (1, D), F32)
        rowbf = sb("rowbf", (1, D), BF16)
        stg0 = sb("stg0", (128, 1, D), F32)
        stg1 = sb("stg1", (128, 1, D), F32)
        stgbf = [sb("stgbf0", (128, 1, D), BF16), sb("stgbf1", (128, 1, D), BF16)]

        with contextlib.ExitStack() as ssA:
            def sa(name, shape, dt):
                return ssA.enter_context(nc.sbuf_tensor(name, list(shape), dt))
            xc32 = [sa("xc32_0", (128, D), F32), sa("xc32_1", (128, D), F32)]
            xnb = sa("xnb", (128, D), BF16)
            q32 = sa("q32", (128, D), F32)
            k32 = sa("k32", (128, D), F32)
            g32 = sa("g32", (128, D), F32)
            o32 = sa("o32", (128, D), F32)
            qrb = sa("qrb", (128, D), BF16)
            krb = sa("krb", (128, D), BF16)
            vb = sa("vb", (128, D), BF16)
            apb = sa("apb", (128, D), BF16)
            qfm = sa("qfm", (128, KB, 128), BF16)
            kfm = sa("kfm", (128, KB, 128), BF16)
            apT = sa("apT", (128, KB, 128), BF16)
            cosS = sa("cosS", (128, 512), F32)
            sinS = sa("sinS", (128, 512), F32)
            tr2 = sa("tr2", (128, 512), F32)
            tr3 = sa("tr3", (128, 512), F32)
            wkb = sa("wkb", (128, HM), BF16)
            wvb = sa("wvb", (128, HM), BF16)
            wvT = [sa(f"wvT{h}", (64, 128), BF16) for h in range(H)]
            aTbk = sa("aTbk", (16, 128), BF16)
            aTbv = sa("aTbv", (16, 128), BF16)
            lzs = sa("lzs", (128, M), F32)
            e_s = sa("e_s", (128, M), F32)
            zkr = sa("zkr", (128, M), F32)
            dnr = sa("dnr", (128, M), F32)
            ptb = sa("ptb", (64, 128), BF16)
            ptb2 = sa("ptb2", (64, 128), BF16)
            Smb = sa("Smb", (128, 128), BF16)
            Smb2 = sa("Smb2", (128, 128), BF16)
            AmT = sa("AmT", (128, 128), BF16)
            AmT2 = sa("AmT2", (128, 128), BF16)
            pcol = [sa(f"pcol{i}", (128, 1), F32) for i in range(8)]
            th32 = sa("th32", (128, DKH), F32)
            gg32 = sa("gg32", (128, DKH), F32)


        with contextlib.ExitStack() as ssB:
            def sbB(name, shape, dt):
                return ssB.enter_context(nc.sbuf_tensor(name, list(shape), dt))
            Y = sbB("Y", (128, NCH, D), F32)
            hfm = sbB("hfm", (128, KB, TIL), BF16)
            hsrc = sbB("hsrc", (128, TIL), F32)
            x3s = sbB("x3s", (128, D), F32)
            obc = sbB("obc", (128, TIL), F32)
            scf = sbB("scf", (128, E), F32)
            sc2 = sbB("sc2", (128, E), F32)
            mks = sbB("mks", (128, E), F32)
            inv_m = sbB("invm", (128, E), F32)
            omg = sbB("omg", (128, NCH, E), F32)
            rhb = sbB("rhb", (128, E), BF16)
            ohb = sbB("ohb", (128, E), BF16)
            oT = sbB("oT", (8, NCH, 128), BF16)
            rT = sbB("rT", (8, NCH, 128), BF16)
            b3pad = sbB("b3pad", (8, D), BF16)
            b2row = sbB("b2row", (1, D), BF16)
            b1cs = sbB("b1cs", (128, KB), F32)
            rcol = [sbB(f"rcol{i}", (128, 1), F32) for i in range(4)]


        with contextlib.ExitStack() as psA:
            def pa_(name, shape, dt):
                return psA.enter_context(nc.psum_tensor(name, list(shape), dt))
            pa0 = pa_("pa0", (128, 512), F32)
            pa1 = pa_("pa1", (128, 512), F32)
            pb2a = pa_("pb2a", (128, 1024), BF16)
            pb2b = pa_("pb2b", (128, 1024), BF16)
            pc0 = pa_("pc0", (128, 512), F32)
            pc1 = pa_("pc1", (128, 512), F32)
            pd0 = pa_("pd0", (128, 512), F32)
            pe0 = pa_("pe0", (128, 512), F32)
        with contextlib.ExitStack() as psB:
            def pb_(name, shape, dt):
                return psB.enter_context(nc.psum_tensor(name, list(shape), dt))
            qa0 = pb_("qa0", (128, 512), F32)
            qa1 = pb_("qa1", (128, 512), F32)
            qb2 = pb_("qb2", (128, 1024), BF16)
            qc0 = pb_("qc0", (128, 512), F32)
            qc1 = pb_("qc1", (128, 512), F32)
            qd0 = pb_("qd0", (128, 512), F32)
            qd1 = pb_("qd1", (128, 512), F32)
            qe0 = pb_("qe0", (128, 512), F32)

        R = {}
        def res(name):
            if name not in R:
                R[name] = Res(name)
            return R[name]
        for n in ("XT", "KS", "KSb", "VS0", "VS1", "VS2", "VS3",
                  "VSb0", "VSb1", "VSb2", "VSb3", "zkS", "zvS",
                  "pa0", "pa1", "pb2a", "pb2b", "pc0", "pc1", "pd0", "pe0",
                  "qa0", "qa1", "qb2", "qc0", "qc1", "qd0", "qd1", "qe0",
                  "xc0", "xc1", "xm32", "xnb", "wstb", "q32", "k32", "g32",
                  "o32", "ob", "qrb", "krb", "vb", "apb", "qfm", "kfm", "apT",
                  "cosS", "sinS", "tr1", "tr2", "tr3", "tr4",
                  "wkb", "wvb", "wvT0", "wvT1", "wvT2", "wvT3", "skc", "svc", "aTbk", "aTbv",
                  "lzs", "e_s", "zkr", "dnr", "ptb", "ptb2", "Smb", "Smb2",
                  "AmT", "AmT2",
                  "th32", "gg32", "gh32", "onw32", "stgA",
                  "Y", "hfm", "hsrc", "x3s", "cmb", "obc", "scf", "sc2",
                  "mks", "invm", "omg", "rhb", "ohb", "oT", "rT",
                  "b3pad", "b2row", "b1cs", "stgB",
                  "smallstg", "rowstg", "rowbf", "wbfa", "wbfm", "stg0", "stg1",
                  "stg2", "stg3", "stgbf0", "stgbf1",
                  "s0", "s1", "s2", "s3", "s4", "s5",
                  "yhbm"):
            res(n)
        for i in range(8):
            res(f"pcol{i}")
        for i in range(4):
            res(f"rcol{i}")

        slotres = [res(f"s{i}") for i in range(6)]
        wbfm_r = [res(f"wbfm{i}") for i in range(NEP)]
        slotsem = ["ds0", "ds1", "ds2", "ds3", "ds4", "ds5"]
        pcolr = [res(f"pcol{i}") for i in range(8)]
        rcolr = [res(f"rcol{i}") for i in range(4)]

        def dma(queue_eng, dst, src, reads=(), writes=(), sem_="dsmall", waits=()):
            return B.g(queue_eng, lambda e, _d=dst, _s=src: e.dma_start(out=_d, in_=_s),
                       reads=reads, writes=writes, waits=waits, inc=sem_)

        dma("sp", utfs[:], utfs_h)
        dma("sp", utb[:], utb_h)
        dma("sp", idf[:], idf_h)
        dma("sp", idb[:], idb_h)
        dma("sp", onesrf[:], onesrf_h)
        dma("sp", onesrb[:], onesrb_h)
        dma("sp", onescb[:], onescb_h)
        dma("sp", onehot[:], onehot_h.rearrange("e k t -> k e t"))
        dma("sp", zrow[:], zrow_h)
        dma("sp", cpool[:], cpool_h)
        dma("sp", svb2c[:], svb2c_h)
        m_consts = B.ms("dsmall")

        v_skw1 = stg0[:, 0, 0:128].rearrange("p (k l) -> p k l", k=KB)
        v_svw1 = stg0[:, 0, 128:256].rearrange("p (k l) -> p k l", k=KB)
        v_skw2 = stg0[0:16, 0, 256:512]
        v_svw2 = stg0[0:16, 0, 512:768]
        v_gate = stg1[:, 0, 0:KB * E].rearrange("p (k e) -> p k e", k=KB)
        v_skb2 = stg1[0:1, 0, 64:64 + HM]
        v_svb2 = stg1[0:1, 0, 320:320 + HM]
        v_gnw = stg1[0:1, 0, 576:576 + DKH]
        dma("sp", v_skw1, skw1_h, writes=[res("stg0")])
        dma("sp", v_svw1, svw1_h, writes=[res("stg0")])
        dma("sp", v_skb2, skb2_h, writes=[res("stg1")])
        dma("sp", v_svb2, svb2_h, writes=[res("stg1")])
        dma("sp", v_skw2, skw2_h, writes=[res("stg0")])
        dma("sp", v_svw2, svw2_h, writes=[res("stg0")])
        dma("sp", v_gate, gateT_h, writes=[res("stg1")])
        dma("sp", v_gnw, gnw_h, writes=[res("stg1")])
        m_allsmall = B.ms("dsmall")
        B.g("pool", lambda e: e.tensor_copy(out=skw1s[:], in_=v_skw1),
            reads=[res("stg0")], waits=[m_allsmall])
        B.g("pool", lambda e: e.tensor_copy(out=svw1s[:], in_=v_svw1),
            reads=[res("stg0")], waits=[m_allsmall])
        B.g("act", lambda e: e.copy(out=skb2r[:], in_=v_skb2),
            reads=[res("stg1")], waits=[m_allsmall])
        B.g("act", lambda e: e.copy(out=svb2r[:], in_=v_svb2),
            reads=[res("stg1")], waits=[m_allsmall])
        B.g("pool", lambda e: e.tensor_copy(out=skw2s[:], in_=v_skw2),
            reads=[res("stg0")], waits=[m_allsmall])
        B.g("pool", lambda e: e.tensor_copy(out=svw2s[:], in_=v_svw2),
            reads=[res("stg0")], waits=[m_allsmall])
        B.g("pool", lambda e: e.tensor_copy(out=gts[:], in_=v_gate),
            reads=[res("stg1")], waits=[m_allsmall])

        def gnw_mm(e):
            return e.matmul(pe0[:, 0:DKH], onesrf[:], v_gnw,
                            start=True, stop=True)
        B.g("pe", gnw_mm, reads=[res("stg1")], writes=[res("pe0")],
            waits=[m_allsmall])
        B.g("dve", lambda e: e.tensor_copy(out=gnwbc[:], in_=pe0[:, 0:DKH]),
            reads=[res("pe0")])

        B.g("dve", lambda e: e.memset(KS[:], 0.0), writes=[res("KS")])
        B.g("dve", lambda e: e.memset(KSb[:], 0.0), writes=[res("KSb")])
        for h in range(H):
            B.g("dve", lambda e, _h=h: e.memset(VS[_h][:], 0.0),
                writes=[res(f"VS{h}")])
            B.g("dve", lambda e, _h=h: e.memset(VSb[_h][:], 0.0),
                writes=[res(f"VSb{h}")])
        B.g("dve", lambda e: e.memset(zkS[:], 0.0), writes=[res("zkS")])
        B.g("dve", lambda e: e.memset(zvS[:], 0.0), writes=[res("zvS")])
        B.barrier(extra=[B.ms("dsmall")])

        AW_SRC = [wq_h, wk_h, wv_h, wg_h, wo_h]

        def load_weight_matrix(slot_t, slot_r, slot_sem, src_f32, src_bf,
                               first, wbsem, stg_t, stg_r, wb_r, stg_sem):
            if first:
                nstg = len(stg_t)
                qmap = ("sp", "pool")
                wqmap = ("pool",)
                wsems = (wbsem,)
                for kb in range(KB):
                    st_t, st_r, st_s = (stg_t[kb % nstg], stg_r[kb % nstg],
                                        stg_sem[kb % nstg])
                    rows = src_f32[kb * 128:(kb + 1) * 128, :]
                    B.g(qmap[kb % nstg],
                        lambda e, _d=st_t[:, 0, :], _s=rows:
                        e.dma_start(out=_d, in_=_s),
                        writes=[st_r], inc=st_s)
                    B.g("pool", lambda e, _kb=kb, _sl=slot_t, _st=st_t:
                        e.tensor_copy(out=_sl[:, _kb, :], in_=_st[:, 0, :]),
                        reads=[st_r], writes=[slot_r])
                    B.g("pool",
                        lambda e, _kb=kb, _sl=slot_t, _wb=src_bf:
                        e.dma_start(out=_wb[_kb * 128:(_kb + 1) * 128, :],
                                    in_=_sl[:, _kb, :]),
                        reads=[slot_r], writes=[wb_r],
                        inc=wsems[kb % len(wsems)])
            else:
                for pr in range(4):
                    rows = src_bf[pr * 256:(pr + 1) * 256, :]
                    B.g("sp",
                        lambda e, _sl=slot_t, _pr=pr, _s=rows.rearrange(
                        "(k p) n -> p k n", p=128): e.dma_start(
                            out=_sl[:, _pr * 2:(_pr + 1) * 2, :], in_=_s),
                        reads=[wb_r], writes=[slot_r], inc=slot_sem)

        def precast_matrix(src_f32, wb_dst, wb_r, pi):
            for kb in range(KB):
                par = (pi * KB + kb) % 2
                st_t, st_r = (stg0, res("stg0")) if par == 0 else (stg1, res("stg1"))
                sb_t, sb_r = stgbf[par], res(f"stgbf{par}")
                rows = src_f32[kb * 128:(kb + 1) * 128, :]
                B.g("pool", lambda e, _d=st_t[:, 0, :], _s=rows:
                    e.dma_start(out=_d, in_=_s),
                    writes=[st_r], inc=("dstga" if par == 0 else "dstgb"))
                B.g("pool", lambda e, _st=st_t, _sb=sb_t:
                    e.tensor_copy(out=_sb[:, 0, :], in_=_st[:, 0, :]),
                    reads=[st_r], writes=[sb_r])
                B.g("pool", lambda e, _kb=kb, _sb=sb_t, _wb=wb_dst:
                    e.dma_start(out=_wb[_kb * 128:(_kb + 1) * 128, :],
                                in_=_sb[:, 0, :]),
                    reads=[sb_r], writes=[wb_r], inc="dwm")

        CORE_BANK = [res("pc0"), res("pc1")]
        CORE_T = [pc0, pc1]

        def phase_a(p):
            first = (p == 0)
            if first:
                for wi in range(5):
                    load_weight_matrix(SLOT[wi], slotres[wi], slotsem[wi],
                                       AW_SRC[wi], wbf_attn[wi],
                                       True, "dwa", [stg0, stg1],
                                       [res("stg0"), res("stg1")],
                                       res("wbfa"), ["dstga", "dstgb"])
            prep_p1(p, 0)
            prep_sk(0)
            prep_qk(p, 0)
            prep_vg(0)
            for c in range(NCH):
                head_pair(c, 0, 1)
                if c + 1 < NCH:
                    prep_p1(p, c + 1)
                head_pair(c, 2, 3)
                chunk_states(c)
                chunk_wo(p, c)
                if c + 1 < NCH:
                    prep_sk(c + 1)
                    prep_qk(p, c + 1)
                    prep_vg(c + 1)

        def prep_p1(p, c):
            toff = p * QT + c * CH
            par = c % 2
            xcr = res(f"xc{par}")
            xct = xc32[par]
            dsem = "dxa" if par == 0 else "dxb"
            B.g("sp", lambda e, _d=xct[:], _s=x_h[toff:toff + CH, :]:
                e.dma_start(out=_d, in_=_s), writes=[xcr], inc=dsem)
            B.g("sp", lambda e, _d=cosS[:], _s=cos4_h[toff:toff + CH, :]:
                e.dma_start(out=_d, in_=_s), writes=[res("cosS")], inc="dcos")
            B.g("sp", lambda e, _d=sinS[:], _s=sin4_h[toff:toff + CH, :]:
                e.dma_start(out=_d, in_=_s), writes=[res("sinS")], inc="dsin")
            B.g("sp", lambda e, _d=tr2[:], _s=cos4l_h[toff:toff + CH, :]:
                e.dma_start(out=_d, in_=_s), writes=[res("tr2")], inc="dcl")
            B.g("sp", lambda e, _d=tr3[:], _s=sin4l_h[toff:toff + CH, :]:
                e.dma_start(out=_d, in_=_s), writes=[res("tr3")], inc="dsl")
            B.g("pool", lambda e: e.tensor_tensor(
                out=cosS[:], in0=cosS[:], in1=tr2[:], op=ALU.add),
                reads=[res("tr2"), res("cosS")], writes=[res("cosS")])
            B.g("pool", lambda e: e.tensor_tensor(
                out=sinS[:], in0=sinS[:], in1=tr3[:], op=ALU.add),
                reads=[res("tr3"), res("sinS")], writes=[res("sinS")])
            B.g("dve", lambda e: e.reduce_sum(out=pcol[1][:], in_=xct[:], axis=AX.X),
                reads=[xcr], writes=[pcolr[1]])
            B.g("dve", lambda e: e.tensor_scalar(
                out=pcol[2][:], in0=pcol[1][:], scalar1=-1.0 / D, scalar2=None,
                op0=ALU.mult), reads=[pcolr[1]], writes=[pcolr[2]])
            B.g("dve", lambda e: e.tensor_scalar(
                out=q32[:], in0=xct[:], scalar1=pcol[2][:], scalar2=None,
                op0=ALU.add), reads=[xcr, pcolr[2]], writes=[res("q32")])
            B.g("act", lambda e: e.activation(
                out=xnb[:], in_=q32[:], func=AF.Square, accum_out=pcol[3][:]),
                reads=[res("q32")], writes=[res("xnb"), pcolr[3]])
            B.g("act", lambda e: e.activation(
                out=pcol[4][:], in_=pcol[3][:], func=AF.Ln, scale=1.0 / D,
                bias=LN_EPS), reads=[pcolr[3]], writes=[pcolr[4]])
            B.g("act", lambda e: e.activation(
                out=pcol[4][:], in_=pcol[4][:], func=AF.Exp, scale=-0.5),
                reads=[pcolr[4]], writes=[pcolr[4]])
            B.g("dve", lambda e: e.tensor_scalar(
                out=xnb[:], in0=q32[:], scalar1=pcol[4][:], scalar2=None,
                op0=ALU.mult), reads=[res("q32"), pcolr[4]], writes=[res("xnb")])
            def xnT(e):
                last = None
                for kb in range(KB):
                    last = e.transpose(pb2a[:, kb * 128:(kb + 1) * 128],
                                       xnb[:, kb * 128:(kb + 1) * 128], idb[:])
                return last
            B.g("pe", xnT, reads=[res("xnb")], writes=[res("pb2a")],
                waits=[m_consts])
            B.g("act", lambda e, _c=c: e.copy(
                out=XT[:, :, _c * CH:(_c + 1) * CH],
                in_=pb2a[:].rearrange("p (k t) -> p k t", k=KB)),
                reads=[res("pb2a")], writes=[res("XT")])

        def prep_sk(c):
            def aT_k(e):
                last = None
                for kb in range(KB):
                    last = e.matmul(pe0[0:16, 0:128], skw1s[:, kb, :],
                                    XT[:, kb, c * CH:(c + 1) * CH],
                                    start=(kb == 0), stop=(kb == KB - 1))
                return last
            B.g("pe", aT_k, reads=[res("XT")], writes=[res("pe0")])
            B.g("dve", lambda e: e.tensor_copy(out=aTbk[:], in_=pe0[0:16, 0:128]),
                reads=[res("pe0")], writes=[res("aTbk")])

            def aT_v(e):
                last = None
                for kb in range(KB):
                    last = e.matmul(pe0[0:16, 128:256], svw1s[:, kb, :],
                                    XT[:, kb, c * CH:(c + 1) * CH],
                                    start=(kb == 0), stop=(kb == KB - 1))
                return last
            B.g("pe", aT_v, reads=[res("XT")], writes=[res("pe0")])
            B.g("dve", lambda e: e.tensor_copy(out=aTbv[:], in_=pe0[0:16, 128:256]),
                reads=[res("pe0")], writes=[res("aTbv")])

            def sk_mm(e):
                e.matmul(pc0[:, 0:HM], aTbk[:], skw2s[:], start=True, stop=False)
                return e.matmul(pc0[:, 0:HM], onesrb[:], skb2r[:],
                                start=False, stop=True)
            B.g("pe", sk_mm, reads=[res("aTbk")], writes=[res("pc0")])
            B.g("dve", lambda e: e.tensor_scalar(
                out=k32[:, 0:HM], in0=pc0[:, 0:HM], scalar1=CLAMP, scalar2=-CLAMP,
                op0=ALU.min, op1=ALU.max), reads=[res("pc0")], writes=[res("k32")])
            B.g("act", lambda e: e.activation(out=wkb[:], in_=k32[:, 0:HM],
                                              func=AF.Exp),
                reads=[res("k32")], writes=[res("wkb")])

            def sv_mm(e):
                e.matmul(pc1[:, 0:HM], aTbv[:], svw2s[:], start=True, stop=False)
                return e.matmul(pc1[:, 0:HM], onesrb[:], svb2r[:],
                                start=False, stop=True)
            B.g("pe", sv_mm, reads=[res("aTbv")], writes=[res("pc1")])
            B.g("dve", lambda e: e.tensor_scalar(
                out=k32[:, HM:2 * HM], in0=pc1[:, 0:HM], scalar1=CLAMP,
                scalar2=-CLAMP, op0=ALU.min, op1=ALU.max),
                reads=[res("pc1")], writes=[res("k32")])
            B.g("act", lambda e: e.activation(out=wvb[:], in_=k32[:, HM:2 * HM],
                                              func=AF.Exp),
                reads=[res("k32")], writes=[res("wvb")])

            def svT_mm(e):
                last = None
                for h in range(H):
                    last = e.matmul(pd0[0:64, h * 128:(h + 1) * 128],
                                    svw2s[:, h * 64:(h + 1) * 64], aTbv[:],
                                    start=True, stop=True)
                return last
            B.g("pe", svT_mm, reads=[res("aTbv")], writes=[res("pd0")])
            for h in range(H):
                B.g("dve", lambda e, _h=h: e.tensor_scalar(
                    out=q32[0:64, _h * 128:(_h + 1) * 128],
                    in0=pd0[0:64, _h * 128:(_h + 1) * 128],
                    scalar1=svb2c[:, _h:_h + 1], scalar2=CLAMP,
                    op0=ALU.add, op1=ALU.min),
                    reads=[res("pd0")], writes=[res("q32")])
            B.g("dve", lambda e: e.tensor_scalar(
                out=q32[0:64, 0:512], in0=q32[0:64, 0:512], scalar1=-CLAMP,
                scalar2=None, op0=ALU.max), reads=[res("q32")], writes=[res("q32")])
            for h in range(H):
                B.g("act", lambda e, _h=h: e.activation(
                    out=wvT[_h][:], in_=q32[0:64, _h * 128:(_h + 1) * 128],
                    func=AF.Exp), reads=[res("q32")], writes=[res(f"wvT{h}")])

        def _proj(c, name, dst32, dstres, tobf=None):
            wslots = {"q": (SLOT[0], slotres[0]), "k": (SLOT[1], slotres[1]),
                      "v": (SLOT[2], slotres[2]), "g": (SLOT[3], slotres[3])}
            slott, slotr = wslots[name]
            for half in range(2):
                bank = res("pa0") if half == 0 else res("pa1")
                bankt = pa0 if half == 0 else pa1

                def mmf(e, _sl=slott, _h=half, _bk=bankt, _c=c):
                    last = None
                    for kb in range(KB):
                        last = e.matmul(_bk[:], XT[:, kb, _c * CH:(_c + 1) * CH],
                                        _sl[:, kb, _h * 512:(_h + 1) * 512],
                                        start=(kb == 0), stop=(kb == KB - 1))
                    return last
                B.g("pe", mmf, reads=[res("XT"), slotr], writes=[bank])
                tgt = tobf if tobf is not None else dst32
                B.g("act", lambda e, _h=half, _bk=bankt, _d=tgt:
                    e.copy(out=_d[:, _h * 512:(_h + 1) * 512], in_=_bk[:]),
                    reads=[bank], writes=[dstres])

        def _rope(src32, srcr, dstb, dstr):
            sv = src32[:].rearrange("p (h x) -> p h x", h=H)
            dv = dstb[:].rearrange("p (h x) -> p h x", h=H)
            s1 = sv[:, :, 0:128]
            s2 = sv[:, :, 128:256]
            d1 = dv[:, :, 0:128]
            d2 = dv[:, :, 128:256]
            cosv = cosS[:].rearrange("p (h d) -> p h d", h=H)
            sinv = sinS[:].rearrange("p (h d) -> p h d", h=H)
            t2v = tr2[:].rearrange("p (h d) -> p h d", h=H)
            t3v = tr3[:].rearrange("p (h d) -> p h d", h=H)
            B.g("pool", lambda e: e.tensor_tensor(
                out=t2v, in0=s1, in1=cosv, op=ALU.mult),
                reads=[srcr, res("cosS")], writes=[res("tr2")])
            B.g("pool", lambda e: e.tensor_tensor(
                out=t3v, in0=s2, in1=sinv, op=ALU.mult),
                reads=[srcr, res("sinS")], writes=[res("tr3")])
            B.g("pool", lambda e: e.tensor_tensor(
                out=d1, in0=t2v, in1=t3v, op=ALU.subtract),
                reads=[res("tr2"), res("tr3")], writes=[dstr])
            B.g("pool", lambda e: e.tensor_tensor(
                out=t2v, in0=s2, in1=cosv, op=ALU.mult),
                reads=[srcr, res("cosS")], writes=[res("tr2")])
            B.g("pool", lambda e: e.tensor_tensor(
                out=t3v, in0=s1, in1=sinv, op=ALU.mult),
                reads=[srcr, res("sinS")], writes=[res("tr3")])
            B.g("pool", lambda e: e.tensor_tensor(
                out=d2, in0=t2v, in1=t3v, op=ALU.add),
                reads=[res("tr2"), res("tr3")], writes=[dstr])

        def prep_qk(p, c):
            _proj(c, "q", q32, res("q32"))
            _proj(c, "k", k32, res("k32"))
            _rope(q32, res("q32"), qrb, res("qrb"))
            _rope(k32, res("k32"), krb, res("krb"))

            def tq(e):
                last = None
                for kb in range(KB):
                    last = e.transpose(pb2b[:, kb * 128:(kb + 1) * 128],
                                       qrb[:, kb * 128:(kb + 1) * 128], idb[:])
                return last
            B.g("pe", tq, reads=[res("qrb")], writes=[res("pb2b")])
            B.g("dve", lambda e: e.tensor_copy(
                out=qfm[:].rearrange("p k t -> p (k t)"), in_=pb2b[:]),
                reads=[res("pb2b")], writes=[res("qfm")])

            def tk(e):
                last = None
                for kb in range(KB):
                    last = e.transpose(pb2a[:, kb * 128:(kb + 1) * 128],
                                       krb[:, kb * 128:(kb + 1) * 128], idb[:])
                return last
            B.g("pe", tk, reads=[res("krb")], writes=[res("pb2a")])
            B.g("dve", lambda e: e.tensor_copy(
                out=kfm[:].rearrange("p k t -> p (k t)"), in_=pb2a[:]),
                reads=[res("pb2a")], writes=[res("kfm")])

        def prep_vg(c):
            _proj(c, "v", None, res("vb"), tobf=vb)
            _proj(c, "g", g32, res("g32"))

        def head_pair(c, a, b):
            ctx = {}
            for x, (bk, bkt, smb, smr, ptt, ptr, amt, amr, zc, zr, mc, mr,
                    psl) in (
                (a, (res("pc0"), pc0, Smb, res("Smb"), ptb, res("ptb"),
                     AmT, res("AmT"), pcol[0], pcolr[0], pcol[5], pcolr[5],
                     (0, 64))),
                (b, (res("pc1"), pc1, Smb2, res("Smb2"), ptb2, res("ptb2"),
                     AmT2, res("AmT2"), pcol[6], pcolr[6], pcol[7], pcolr[7],
                     (64, 128))),
            ):
                ctx[x] = dict(bank=bk, bkt=bkt, smb=smb, smr=smr, ptt=ptt,
                              ptr=ptr, amt=amt, amr=amr, zc=zc, zr=zr, mc=mc,
                              mr=mr, psl=psl)

            def smm(x):
                t = ctx[x]

                def f(e, _h=x, _b=t["bkt"]):
                    e.matmul(_b[:, 0:128], kfm[:, 2 * _h, :], qfm[:, 2 * _h, :],
                             start=True, stop=False)
                    return e.matmul(_b[:, 0:128], kfm[:, 2 * _h + 1, :],
                                    qfm[:, 2 * _h + 1, :], start=False, stop=True)
                B.g("pe", f, reads=[res("kfm"), res("qfm")], writes=[t["bank"]])

            def smask(x):
                t = ctx[x]
                B.g("dve", lambda e, _b=t["bkt"], _s=t["smb"]: e.tensor_tensor(
                    out=_s[:], in0=_b[:, 0:128], in1=utfs[:], op=ALU.mult),
                    reads=[t["bank"]], writes=[t["smr"]])

            def lmm(x):
                t = ctx[x]

                def f(e, _h=x, _b=t["bkt"], _s=t["smb"]):
                    e.matmul(_b[:, 128:192], _s[:], wkb[:, _h * 64:(_h + 1) * 64],
                             start=True, stop=False)
                    e.matmul(_b[:, 128:192], qfm[:, 2 * _h, :],
                             KSb[:, (2 * _h) * 64:(2 * _h + 1) * 64],
                             start=False, stop=False)
                    return e.matmul(_b[:, 128:192], qfm[:, 2 * _h + 1, :],
                                    KSb[:, (2 * _h + 1) * 64:(2 * _h + 2) * 64],
                                    start=False, stop=True)
                B.g("pe", f, reads=[t["smr"], res("qfm"), res("wkb"),
                                    res("KSb")], writes=[t["bank"]])

            def zmm(x):
                t = ctx[x]

                def f(e, _h=x, _b=t["bkt"]):
                    e.matmul(_b[:, 192:256], utb[:], wkb[:, _h * 64:(_h + 1) * 64],
                             start=True, stop=False)
                    e.matmul(_b[:, 192:256], onesrf[:],
                             zkS[0:1, _h * 64:(_h + 1) * 64], start=False, stop=True)
                    e.matmul(_b[:, 256:320], utb[:], wvb[:, _h * 64:(_h + 1) * 64],
                             start=True, stop=False)
                    return e.matmul(_b[:, 256:320], onesrf[:],
                                    zvS[0:1, _h * 64:(_h + 1) * 64],
                                    start=False, stop=True)
                B.g("pe", f, reads=[res("wkb"), res("wvb"), res("zkS"),
                                    res("zvS")], writes=[t["bank"]])

            def soft(x):
                t = ctx[x]
                bkt_ = t["bkt"]
                B.g("dve", lambda e: e.reciprocal(out=zkr[:], in_=bkt_[:, 192:256]),
                    reads=[t["bank"]], writes=[res("zkr")])
                B.g("dve", lambda e: e.tensor_tensor(
                    out=lzs[:], in0=bkt_[:, 128:192], in1=zkr[:], op=ALU.mult),
                    reads=[t["bank"], res("zkr")], writes=[res("lzs")])
                B.g("dve", lambda e, _m=t["mc"]: e.reduce_max(
                    out=_m[:], in_=lzs[:], axis=AX.X, negate=True),
                    reads=[res("lzs")], writes=[t["mr"]])
                B.g("act", lambda e, _m=t["mc"], _z=t["zc"]: e.activation(
                    out=e_s[:], in_=lzs[:], func=AF.Exp, bias=_m[:],
                    scale=1.0, accum_out=_z[:]),
                    reads=[res("lzs"), t["mr"]], writes=[res("e_s"), t["zr"]])
                B.g("act", lambda e, _z=t["zc"]: e.activation(
                    out=dnr[:], in_=bkt_[:, 256:320], func=AF.Copy,
                    scale=_z[:]), reads=[t["bank"], t["zr"]], writes=[res("dnr")])
                B.g("dve", lambda e: e.reciprocal(out=dnr[:], in_=dnr[:]),
                    reads=[res("dnr")], writes=[res("dnr")])
                lo, hi = t["psl"]
                B.g("dve", lambda e, _l=lo, _h=hi: e.tensor_tensor(
                    out=xnb[:, _l:_h], in0=e_s[:], in1=dnr[:], op=ALU.mult),
                    reads=[res("e_s"), res("dnr")], writes=[res("xnb")])

            def ptT(x):
                t = ctx[x]
                lo, hi = t["psl"]
                B.g("pe", lambda e, _l=lo, _h=hi: e.transpose(
                    pb2b[0:64, _l * 2:_l * 2 + 128], xnb[:, _l:_h], idb[:]),
                    reads=[res("xnb")], writes=[res("pb2b")])
                B.g("dve", lambda e, _l=lo, _p=t["ptt"]: e.tensor_copy(
                    out=_p[:], in_=pb2b[0:64, _l * 2:_l * 2 + 128]),
                    reads=[res("pb2b")], writes=[t["ptr"]])

            def amm(x):
                t = ctx[x]
                B.g("pe", lambda e, _h=x, _b=t["bkt"], _p=t["ptt"]: e.matmul(
                    _b[:, 320:448], wvT[_h][:], _p[:], start=True, stop=True),
                    reads=[res(f"wvT{x}"), t["ptr"]], writes=[t["bank"]])

            def amask(x):
                t = ctx[x]
                B.g("dve", lambda e, _b=t["bkt"], _a=t["amt"]: e.tensor_tensor(
                    out=_a[:], in0=_b[:, 320:448], in1=utfs[:], op=ALU.mult),
                    reads=[t["bank"]], writes=[t["amr"]])

            def omm(x):
                t = ctx[x]

                def f(e, _h=x, _b=t["bkt"], _a=t["amt"], _p=t["ptt"]):
                    e.matmul(_b[:, 0:256], _a[:], vb[:, _h * 256:(_h + 1) * 256],
                             start=True, stop=False)
                    return e.matmul(_b[:, 0:256], _p[:], VSb[_h][:],
                                    start=False, stop=True)
                B.g("pe", f, reads=[t["amr"], res("vb"), t["ptr"],
                                    res(f"VSb{x}")], writes=[t["bank"]])

            def oevac(x):
                t = ctx[x]
                B.g("act", lambda e, _h=x, _b=t["bkt"]: e.copy(
                    out=o32[:, _h * 256:(_h + 1) * 256], in_=_b[:, 0:256]),
                    reads=[t["bank"]], writes=[res("o32")])

            def tail(x):
                t = ctx[x]
                mc, mr = t["mc"], t["mr"]
                B.g("act", lambda e, _h=x, _m=mc: e.activation(
                    out=xnb[:, 256:512], in_=o32[:, _h * 256:(_h + 1) * 256],
                    func=AF.Square, accum_out=_m[:]),
                    reads=[res("o32")], writes=[res("xnb"), mr])
                B.g("act", lambda e, _m=mc: e.activation(
                    out=_m[:], in_=_m[:], func=AF.Ln, scale=1.0 / DKH,
                    bias=GN_EPS * ATT_SCALE * ATT_SCALE), reads=[mr], writes=[mr])
                B.g("act", lambda e, _m=mc: e.activation(
                    out=_m[:], in_=_m[:], func=AF.Exp, scale=-0.5),
                    reads=[mr], writes=[mr])
                B.g("act", lambda e, _h=x: e.activation(
                    out=th32[:], in_=g32[:, _h * 256:(_h + 1) * 256], func=AF.Exp,
                    scale=-1.0), reads=[res("g32")], writes=[res("th32")])
                B.g("dve", lambda e, _h=x, _m=mc: e.scalar_tensor_tensor(
                    out=gg32[:], in0=o32[:, _h * 256:(_h + 1) * 256],
                    scalar=_m[:], in1=gnwbc[:], op0=ALU.mult, op1=ALU.mult),
                    reads=[res("o32"), mr], writes=[res("gg32")])
                B.g("dve", lambda e, _h=x: e.tensor_tensor(
                    out=gg32[:], in0=gg32[:],
                    in1=g32[:, _h * 256:(_h + 1) * 256], op=ALU.mult),
                    reads=[res("gg32"), res("g32")], writes=[res("gg32")])
                B.g("dve", lambda e: e.tensor_scalar(
                    out=th32[:], in0=th32[:], scalar1=1.0, scalar2=None,
                    op0=ALU.add), reads=[res("th32")], writes=[res("th32")])
                B.g("dve", lambda e: e.reciprocal(out=th32[:], in_=th32[:]),
                    reads=[res("th32")], writes=[res("th32")])
                B.g("dve", lambda e, _h=x: e.tensor_tensor(
                    out=apb[:, _h * 256:(_h + 1) * 256], in0=gg32[:], in1=th32[:],
                    op=ALU.mult),
                    reads=[res("gg32"), res("th32")], writes=[res("apb")])

            smm(a); smm(b)
            smask(a); smask(b)
            lmm(a); zmm(a); lmm(b); zmm(b)
            soft(a)
            ptT(a); amm(a); amask(a)
            soft(b)
            ptT(b); amm(b)
            omm(a); amask(b); omm(b)
            oevac(a); oevac(b)
            tail(a); tail(b)

        def chunk_states(c):
            def ksd(e):
                last = None
                for blk in range(KB):
                    hh = blk // 2
                    last = e.matmul(pd0[:, blk * 64:(blk + 1) * 64],
                                    krb[:, blk * 128:(blk + 1) * 128],
                                    wkb[:, hh * 64:(hh + 1) * 64],
                                    start=True, stop=True)
                return last
            B.g("pe", ksd, reads=[res("krb"), res("wkb")], writes=[res("pd0")])
            B.g("dve", lambda e: e.tensor_tensor(
                out=KS[:], in0=KS[:], in1=pd0[:], op=ALU.add),
                reads=[res("pd0"), res("KS")], writes=[res("KS")])
            B.g("dve", lambda e: e.tensor_scalar(
                out=KSb[:], in0=KS[:], scalar1=ATT_SCALE, scalar2=None,
                op0=ALU.mult), reads=[res("KS")], writes=[res("KSb")])

            def colsums(e):
                e.matmul(pd0[0:1, 0:HM], onescb[:], wkb[:], start=True, stop=True)
                return e.matmul(pd0[0:1, 256:256 + HM], onescb[:], wvb[:],
                                start=True, stop=True)
            B.g("pe", colsums, reads=[res("wkb"), res("wvb")], writes=[res("pd0")])
            B.g("dve", lambda e: e.tensor_tensor(
                out=zkS[:], in0=zkS[:], in1=pd0[0:1, 0:HM], op=ALU.add),
                reads=[res("pd0"), res("zkS")], writes=[res("zkS")])
            B.g("dve", lambda e: e.tensor_tensor(
                out=zvS[:], in0=zvS[:], in1=pd0[0:1, 256:256 + HM], op=ALU.add),
                reads=[res("pd0"), res("zvS")], writes=[res("zvS")])

            for pair in range(2):
                def vsd(e, _p=pair):
                    e.matmul(pd0[0:64, 0:256], wvb[:, (_p * 2) * 64:(_p * 2 + 1) * 64],
                             vb[:, (_p * 2) * 256:(_p * 2 + 1) * 256],
                             start=True, stop=True)
                    return e.matmul(pd0[0:64, 256:512],
                                    wvb[:, (_p * 2 + 1) * 64:(_p * 2 + 2) * 64],
                                    vb[:, (_p * 2 + 1) * 256:(_p * 2 + 2) * 256],
                                    start=True, stop=True)
                B.g("pe", vsd, reads=[res("wvb"), res("vb")], writes=[res("pd0")])
                for j in range(2):
                    hh = pair * 2 + j
                    B.g("dve", lambda e, _h=hh, _j=j: e.tensor_tensor(
                        out=VS[_h][:], in0=VS[_h][:],
                        in1=pd0[0:64, _j * 256:(_j + 1) * 256], op=ALU.add),
                        reads=[res("pd0"), res(f"VS{hh}")], writes=[res(f"VS{hh}")])
                    B.g("dve", lambda e, _h=hh: e.tensor_scalar(
                        out=VSb[_h][:], in0=VS[_h][:], scalar1=ATT_SCALE,
                        scalar2=None, op0=ALU.mult),
                        reads=[res(f"VS{hh}")], writes=[res(f"VSb{hh}")])

        def chunk_wo(p, c):
            toff = p * QT + c * CH
            par = c % 2
            xcr = res(f"xc{par}")
            xct = xc32[par]

            def tap(e):
                last = None
                for kb in range(KB):
                    last = e.transpose(pb2b[:, kb * 128:(kb + 1) * 128],
                                       apb[:, kb * 128:(kb + 1) * 128], idb[:])
                return last
            B.g("pe", tap, reads=[res("apb")], writes=[res("pb2b")])
            B.g("act", lambda e: e.copy(
                out=apT[:].rearrange("p k t -> p (k t)"), in_=pb2b[:]),
                reads=[res("pb2b")], writes=[res("apT")])

            for half in range(2):
                def wo_mm(e, _h=half):
                    last = None
                    for kb in range(KB):
                        last = e.matmul(pe0[:], apT[:, kb, :],
                                        SLOT[4][:, kb, _h * 512:(_h + 1) * 512],
                                        start=(kb == 0), stop=(kb == KB - 1))
                    return last
                B.g("pe", wo_mm, reads=[res("apT"), slotres[4]], writes=[res("pe0")])
                B.g("dve", lambda e, _h=half: e.tensor_tensor(
                    out=o32[:, _h * 512:(_h + 1) * 512], in0=pe0[:],
                    in1=xct[:, _h * 512:(_h + 1) * 512], op=ALU.add),
                    reads=[res("pe0"), xcr], writes=[res("o32")])
            B.g("pool", lambda e, _t=toff: e.dma_start(
                out=y_h[_t:_t + CH, :], in_=o32[:]),
                reads=[res("o32")], writes=[res("yhbm")], inc="dout")

        MOE_SRC = [(w1T_h[e], w3T_h[e], w2T_h[e]) for e in range(E)]
        MOE_SRC.append((sw_h["sw1Ta"], sw_h["sw2Ta"], sw_h["sw3a"]))
        MOE_SRC.append((sw_h["sw1Tb"], sw_h["sw2Tb"], sw_h["sw3b"]))

        def phase_b(p):
            first = (p == 0) or FORCE_F32_STREAM
            qoff = p * QT
            for scg in range(NCH):
                B.g("pool", lambda e, _s=scg: e.dma_start(
                    out=Y[:, _s, :], in_=y_h[qoff + _s * CH:qoff + (_s + 1) * CH, :]),
                    reads=[res("yhbm")], writes=[res("Y")], inc="dyld")
            for scg in range(NCH):
                routing(p, scg)
            if DBG_WBF and p == 1:
                B.g("pool", lambda e: e.dma_start(out=dbg_yep[0], in_=Y[:, 0, :]),
                    reads=[res("Y")], waits=[B.ms("dbq")], inc="dbq")
            for ep in range(NEP):
                exp_pass(p, ep, first)
                if DBG_WBF and p == 1:
                    B.g("pool", lambda e, _ep=ep: e.dma_start(
                        out=dbg_yep[_ep + 1], in_=Y[:, 0, :]),
                        reads=[res("Y")], waits=[B.ms("dbq")], inc="dbq")
            for scg in range(NCH):
                B.g("pool", lambda e, _s=scg: e.dma_start(
                    out=y_h[qoff + _s * CH:qoff + (_s + 1) * CH, :], in_=Y[:, _s, :]),
                    reads=[res("Y")], writes=[res("yhbm")], inc="dout")

        def routing(p, scg):
            def lg(e):
                last = None
                for kb in range(KB):
                    last = e.matmul(qe0[:, 0:E], XT[:, kb, scg * CH:(scg + 1) * CH],
                                    gts[:, kb, :], start=(kb == 0),
                                    stop=(kb == KB - 1))
                return last
            B.g("pe", lg, reads=[res("XT")], writes=[res("qe0")])
            B.g("dve", lambda e: e.reduce_max(out=rcol[0][:], in_=qe0[:, 0:E],
                                              axis=AX.X),
                reads=[res("qe0")], writes=[rcolr[0]])
            B.g("dve", lambda e: e.tensor_scalar(
                out=rcol[0][:], in0=rcol[0][:], scalar1=-1.0, scalar2=None,
                op0=ALU.mult), reads=[rcolr[0]], writes=[rcolr[0]])
            B.g("act", lambda e: e.activation(
                out=scf[:], in_=qe0[:, 0:E], func=AF.Exp, bias=rcol[0][:],
                accum_out=rcol[1][:]),
                reads=[res("qe0"), rcolr[0]], writes=[res("scf"), rcolr[1]])
            B.g("act", lambda e: e.activation(
                out=rcol[1][:], in_=rcol[1][:], func=AF.Copy),
                reads=[rcolr[1]], writes=[rcolr[1]])
            B.g("dve", lambda e: e.reciprocal(out=rcol[1][:], in_=rcol[1][:]),
                reads=[rcolr[1]], writes=[rcolr[1]])
            B.g("dve", lambda e: e.tensor_scalar(
                out=scf[:], in0=scf[:], scalar1=rcol[1][:], scalar2=None,
                op0=ALU.mult), reads=[res("scf"), rcolr[1]], writes=[res("scf")])
            B.g("dve", lambda e: e.reduce_max(out=rcol[2][:], in_=scf[:], axis=AX.X),
                reads=[res("scf")], writes=[rcolr[2]])
            B.g("dve", lambda e: e.tensor_scalar(
                out=sc2[:], in0=scf[:], scalar1=rcol[2][:], scalar2=None,
                op0=ALU.is_equal), reads=[res("scf"), rcolr[2]], writes=[res("sc2")])
            B.g("dve", lambda e: e.tensor_scalar(
                out=sc2[:], in0=sc2[:], scalar1=-1.0, scalar2=1.0,
                op0=ALU.mult, op1=ALU.add), reads=[res("sc2")], writes=[res("sc2")])
            B.g("dve", lambda e: e.tensor_tensor(
                out=sc2[:], in0=scf[:], in1=sc2[:], op=ALU.mult),
                reads=[res("scf"), res("sc2")], writes=[res("sc2")])
            B.g("dve", lambda e: e.reduce_max(out=rcol[3][:], in_=sc2[:], axis=AX.X),
                reads=[res("sc2")], writes=[rcolr[3]])
            B.g("dve", lambda e: e.tensor_scalar(
                out=mks[:], in0=scf[:], scalar1=rcol[3][:], scalar2=None,
                op0=ALU.is_ge), reads=[res("scf"), rcolr[3]], writes=[res("mks")])
            B.g("dve", lambda e, _s=scg: e.tensor_tensor(
                out=omg[:, _s, :], in0=scf[:], in1=mks[:], op=ALU.mult),
                reads=[res("scf"), res("mks")], writes=[res("omg")])
            B.g("dve", lambda e: e.tensor_scalar(
                out=inv_m[:], in0=mks[:], scalar1=-1.0, scalar2=1.0,
                op0=ALU.mult, op1=ALU.add), reads=[res("mks")], writes=[res("invm")])
            B.g("dve", lambda e, _s=scg: e.tensor_tensor(
                out=scf[:], in0=omg[:, _s, :], in1=inv_m[:], op=ALU.add),
                reads=[res("omg"), res("invm")], writes=[res("scf")])
            B.g("dve", lambda e: e.reciprocal(out=scf[:], in_=scf[:]),
                reads=[res("scf")], writes=[res("scf")])
            B.g("dve", lambda e: e.tensor_copy(out=rhb[:], in_=scf[:]),
                reads=[res("scf")], writes=[res("rhb")])
            B.g("dve", lambda e, _s=scg: e.tensor_copy(out=ohb[:],
                                                       in_=omg[:, _s, :]),
                reads=[res("omg")], writes=[res("ohb")])
            B.g("pe", lambda e: e.transpose(qb2[0:E, 0:128], ohb[:], idb[:]),
                reads=[res("ohb")], writes=[res("qb2")])
            B.g("dve", lambda e, _s=scg: e.tensor_copy(
                out=oT[:, _s, :], in_=qb2[0:E, 0:128]),
                reads=[res("qb2")], writes=[res("oT")])
            B.g("pe", lambda e: e.transpose(qb2[0:E, 0:128], rhb[:], idb[:]),
                reads=[res("rhb")], writes=[res("qb2")])
            B.g("dve", lambda e, _s=scg: e.tensor_copy(
                out=rT[:, _s, :], in_=qb2[0:E, 0:128]),
                reads=[res("qb2")], writes=[res("rT")])

        def exp_pass(p, ep, first):
            shared = (ep >= E)
            base = 3 * (ep % 2)
            cw = [SLOT[base + i] for i in range(3)]
            cr = [slotres[base + i] for i in range(3)]
            csem = [slotsem[base + i] for i in range(3)]
            for i in range(3):
                load_weight_matrix(cw[i], cr[i], csem[i], MOE_SRC[ep][i],
                                   wbf_moe[ep, i], first, "dwm",
                                   [stg0, stg1],
                                   [res("stg0"), res("stg1")],
                                   res("wbfm"), ["dstga", "dstgb"])
            if not shared:
                e_idx = ep
                B.g("sp", lambda e: e.dma_start(out=b1cs[:], in_=b1c_h[e_idx]),
                    writes=[res("b1cs")], inc="dbp1")
                B.g("sp", lambda e: e.dma_start(
                    out=rowstg[:], in_=b2r_h[e_idx:e_idx + 1, :]),
                    writes=[res("rowstg")], inc="dbp2")
                B.g("act", lambda e: e.copy(out=b2row[:], in_=rowstg[:]),
                    reads=[res("rowstg")], writes=[res("b2row")])
                B.g("sp", lambda e: e.dma_start(
                    out=rowstg[:], in_=b3r_h[e_idx:e_idx + 1, :]),
                    writes=[res("rowstg")], inc="dbp3")
                B.g("act", lambda e: e.copy(out=rowbf[:], in_=rowstg[:]),
                    reads=[res("rowstg")], writes=[res("rowbf")])
                if e_idx == 0:
                    for r_ in range(1, 8):
                        B.g("pool", lambda e, _r=r_: e.dma_start(
                            out=b3pad[_r:_r + 1, :], in_=zrow[:]),
                            writes=[res("b3pad")], inc="dbp4")
                else:
                    prev = e_idx - 1
                    B.g("pool", lambda e, _p=prev: e.dma_start(
                        out=b3pad[_p:_p + 1, :], in_=zrow[:]),
                        writes=[res("b3pad")], inc="dbp4")
                B.g("pool", lambda e, _e=e_idx: e.dma_start(
                    out=b3pad[_e:_e + 1, :], in_=rowbf[:]),
                    reads=[res("rowbf")], writes=[res("b3pad")], inc="dbp4")
            if DBG_WBF and p == 1 and ep == 0:
                B.g("pool", lambda e, _c=cw[0]: e.dma_start(out=dbg_mslot, in_=_c[:]),
                    reads=[cr[0]], waits=[B.ms("dbq")], inc="dbq")
            if DBG_WBF and p == 1 and ep == 8:
                B.g("pool", lambda e, _c=cw[2]: e.dma_start(out=dbg_sw3, in_=_c[:]),
                    reads=[cr[2]], waits=[B.ms("dbq")], inc="dbq")
            if DBG_WBF and p == 1 and not shared:
                B.g("pool", lambda e, _ep=ep: e.dma_start(out=dbg_b2[_ep], in_=b2row[:]),
                    reads=[res("b2row")], waits=[B.ms("dbq")], inc="dbq")
                B.g("pool", lambda e, _ep=ep: e.dma_start(out=dbg_b3p[_ep], in_=b3pad[:]),
                    reads=[res("b3pad")], waits=[B.ms("dbq")], inc="dbq")
            for t in range(NTIL):
                tile_b(p, ep, t, cw, cr, shared)

        def tile_b(p, ep, t, cw, cr, shared):
            e_idx = ep
            tsl = slice(t * TIL, (t + 1) * TIL)
            if not shared:
                def obmm(e):
                    return e.matmul(qe0[:], onehot[:, e_idx, :],
                                    oT[:, t * NSUB:(t + 1) * NSUB, :].rearrange(
                                        "k a b -> k (a b)"),
                                    start=True, stop=True)
                B.g("pe", obmm, reads=[res("oT")], writes=[res("qe0")])
                B.g("dve", lambda e: e.tensor_copy(out=obc[:], in_=qe0[:]),
                    reads=[res("qe0")], writes=[res("obc")])

            for blk in range(KB):
                bk = [res("qa0"), res("qa1")][blk % 2]
                bkt = [qa0, qa1][blk % 2]
                if not shared:
                    def a1mm(e, _b=blk, _t=bkt):
                        last = None
                        for kb in range(KB):
                            last = e.matmul(_t[:], cw[0][:, kb, _b * 128:(_b + 1) * 128],
                                            XT[:, kb, tsl], start=(kb == 0),
                                            stop=(kb == KB - 1))
                        return last
                    B.g("pe", a1mm, reads=[cr[0], res("XT")], writes=[bk])
                    B.g("dve", lambda e, _t=bkt: e.tensor_tensor(
                        out=hsrc[:], in0=_t[:], in1=obc[:], op=ALU.mult),
                        reads=[bk, res("obc")], writes=[res("hsrc")])
                    B.g("act", lambda e, _b=blk: e.activation(
                        out=hfm[:, _b, :], in_=hsrc[:], func=AF.Silu,
                        bias=b1cs[:, _b:_b + 1], scale=1.0),
                        reads=[res("hsrc"), res("b1cs")], writes=[res("hfm")])
                else:
                    bk2 = [res("qc0"), res("qc1")][blk % 2]
                    bkt2 = [qc0, qc1][blk % 2]

                    def u1mm(e, _b=blk, _t=bkt):
                        last = None
                        for kb in range(KB):
                            last = e.matmul(_t[:], cw[0][:, kb, _b * 128:(_b + 1) * 128],
                                            XT[:, kb, tsl], start=(kb == 0),
                                            stop=(kb == KB - 1))
                        return last
                    B.g("pe", u1mm, reads=[cr[0], res("XT")], writes=[bk])

                    def u2mm(e, _b=blk, _t=bkt2):
                        last = None
                        for kb in range(KB):
                            last = e.matmul(_t[:], cw[1][:, kb, _b * 128:(_b + 1) * 128],
                                            XT[:, kb, tsl], start=(kb == 0),
                                            stop=(kb == KB - 1))
                        return last
                    B.g("pe", u2mm, reads=[cr[1], res("XT")], writes=[bk2])
                    B.g("act", lambda e, _t=bkt: e.activation(
                        out=hsrc[:], in_=_t[:], func=AF.Silu),
                        reads=[bk], writes=[res("hsrc")])
                    B.g("dve", lambda e, _b=blk, _t=bkt2: e.tensor_tensor(
                        out=hfm[:, _b, :], in0=_t[:], in1=hsrc[:], op=ALU.mult),
                        reads=[bk2, res("hsrc")], writes=[res("hfm")])

            for sc in range(NSUB):
                scg = t * NSUB + sc
                ssl = slice(scg * CH, (scg + 1) * CH)
                if not shared:
                    def a3mm(e, _ssl=ssl, _sg=scg):
                        last = None
                        for kb in range(KB):
                            e.matmul(qc0[:], XT[:, kb, _ssl], cw[1][:, kb, 0:512],
                                     start=(kb == 0), stop=False)
                            last = e.matmul(qc1[:], XT[:, kb, _ssl],
                                            cw[1][:, kb, 512:1024],
                                            start=(kb == 0), stop=False)
                        e.matmul(qc0[:], rT[:, _sg, :], b3pad[:, 0:512],
                                 start=False, stop=True)
                        return e.matmul(qc1[:], rT[:, _sg, :], b3pad[:, 512:1024],
                                        start=False, stop=True)
                    B.g("pe", a3mm, reads=[res("XT"), cr[1], res("rT"),
                                           res("b3pad")],
                        writes=[res("qc0"), res("qc1")])
                    B.g("act", lambda e, _s=scg: e.activation(
                        out=x3s[:, 0:512], in_=qc0[:], func=AF.Copy,
                        scale=omg[:, _s, e_idx:e_idx + 1]),
                        reads=[res("qc0"), res("omg")], writes=[res("x3s")])
                    B.g("act", lambda e, _s=scg: e.activation(
                        out=x3s[:, 512:1024], in_=qc1[:], func=AF.Copy,
                        scale=omg[:, _s, e_idx:e_idx + 1]),
                        reads=[res("qc1"), res("omg")], writes=[res("x3s")])
                    def x2mm(e, _sc=sc):
                        last = None
                        for kb in range(KB):
                            e.matmul(qd0[:], hfm[:, kb, _sc * CH:(_sc + 1) * CH],
                                     cw[2][:, kb, 0:512],
                                     start=(kb == 0), stop=False)
                            last = e.matmul(qd1[:],
                                            hfm[:, kb, _sc * CH:(_sc + 1) * CH],
                                            cw[2][:, kb, 512:1024],
                                            start=(kb == 0), stop=False)
                        e.matmul(qd0[:], onesrb[:], b2row[0:1, 0:512],
                                 start=False, stop=True)
                        return e.matmul(qd1[:], onesrb[:], b2row[0:1, 512:1024],
                                        start=False, stop=True)
                    B.g("pe", x2mm, reads=[res("hfm"), cr[2], res("b2row")],
                        writes=[res("qd0"), res("qd1")])
                    for half in range(2):
                        qd = [res("qd0"), res("qd1")][half]
                        qdt = [qd0, qd1][half]
                        B.g("dve", lambda e, _h=half, _t=qdt: e.tensor_tensor(
                            out=x3s[:, _h * 512:(_h + 1) * 512], in0=_t[:],
                            in1=x3s[:, _h * 512:(_h + 1) * 512], op=ALU.mult),
                            reads=[qd], writes=[res("x3s")])
                        B.g("dve", lambda e, _h=half, _s=scg: e.tensor_tensor(
                            out=Y[:, _s, _h * 512:(_h + 1) * 512],
                            in0=Y[:, _s, _h * 512:(_h + 1) * 512],
                            in1=x3s[:, _h * 512:(_h + 1) * 512], op=ALU.add),
                            reads=[res("x3s")], writes=[res("Y")])
                else:
                    def zmm2(e, _sc=sc):
                        last = None
                        for kb in range(KB):
                            e.matmul(qd0[:], hfm[:, kb, _sc * CH:(_sc + 1) * CH],
                                     cw[2][:, kb, 0:512],
                                     start=(kb == 0), stop=(kb == KB - 1))
                            last = e.matmul(qd1[:],
                                            hfm[:, kb, _sc * CH:(_sc + 1) * CH],
                                            cw[2][:, kb, 512:1024],
                                            start=(kb == 0), stop=(kb == KB - 1))
                        return last
                    B.g("pe", zmm2, reads=[res("hfm"), cr[2]],
                        writes=[res("qd0"), res("qd1")])
                    for half in range(2):
                        qd = [res("qd0"), res("qd1")][half]
                        qdt = [qd0, qd1][half]
                        B.g("dve", lambda e, _h=half, _s=scg, _t=qdt: e.tensor_tensor(
                            out=Y[:, _s, _h * 512:(_h + 1) * 512],
                            in0=Y[:, _s, _h * 512:(_h + 1) * 512],
                            in1=_t[:], op=ALU.add),
                            reads=[qd], writes=[res("Y")])

        for p in range(NPASS):
            phase_a(p)
            B.barrier()
            phase_b(p)
            if p + 1 < NPASS:
                for wi in range(5):
                    load_weight_matrix(SLOT[wi], slotres[wi], slotsem[wi],
                                       AW_SRC[wi], wbf_attn[wi],
                                       False, "dwa", None, None,
                                       res("wbfa"), None)
            B.barrier()

        if DBG_WBF:
            B.g("pool", lambda e: e.dma_start(out=dbg_mwbf, in_=wbf_moe[0, 0]),
                reads=[res("wbfm")], waits=[B.ms("dbq")], inc="dbq")
            B.g("pool", lambda e: e.dma_start(out=dbg_sw3c, in_=wbf_moe[8, 2]),
                reads=[res("wbfm")], waits=[B.ms("dbq")], inc="dbq")
            B.g("pool", lambda e: e.dma_start(out=dbg_wbf, in_=wbf_attn[0]),
                reads=[res("wbfa")], waits=[B.ms("dbq")], inc="dbq")
            B.raw("pool", lambda e: e.nop(), [B.ms("dbq")], inc="pool")
        B.raw("sp", lambda e: e.nop(), [B.ms("dout")])

        with nc.Block() as block:
            @block.tensor
            def _(eng):
                for th in B.ops["pe"]:
                    th(eng, sems)

            @block.vector
            def _(eng):
                for th in B.ops["dve"]:
                    th(eng, sems)

            @block.scalar
            def _(eng):
                for th in B.ops["act"]:
                    th(eng, sems)

            @block.gpsimd
            def _(eng):
                for th in B.ops["pool"]:
                    th(eng, sems)

            @block.sync
            def _(eng):
                for th in B.ops["sp"]:
                    th(eng, sems)

    return nc


_NC_CACHE = {}


def kernel(**inputs):
    d = _prep_inputs(inputs)
    c = _host_consts()
    if "nc" not in _NC_CACHE:
        _NC_CACHE["nc"] = build_nc()
    nc = _NC_CACHE["nc"]

    shared = {
        "wq": d["wq"], "wk": d["wk"], "wv": d["wv"], "wg": d["wg"], "wo": d["wo"],
        "sk_w1": d["sk_w1"], "sv_w1": d["sv_w1"],
        "sk_w2": d["sk_w2"], "sv_w2": d["sv_w2"],
        "sk_b2": d["sk_b2"], "sv_b2": d["sv_b2"], "sv_b2c": d["sv_b2c"],
        "gnw": d["gnw"], "gateT": d["gateT"],
        "w1T": d["w1T"], "w2T": d["w2T"], "w3T": d["w3T"],
        "b1c": d["b1c"], "b2r": d["b2r"], "b3r": d["b3r"],
        "sw1Ta": d["sw1Ta"], "sw1Tb": d["sw1Tb"],
        "sw2Ta": d["sw2Ta"], "sw2Tb": d["sw2Tb"],
        "sw3a": d["sw3a"], "sw3b": d["sw3b"],
        "cos4": c["cos4"], "sin4": c["sin4"],
        "cos4l": c["cos4l"], "sin4l": c["sin4l"],
        "ut_fs": c["ut_fs"], "ut_b": c["ut_b"],
        "id_f": c["id_f"], "id_b": c["id_b"],
        "ones_row_f": c["ones_row_f"], "ones_row_b": c["ones_row_b"],
        "ones_col_b": c["ones_col_b"], "onehot_b": c["onehot_b"],
        "zrow_b": c["zrow_b"], "cpool_h": c["cpool"],
    }
    in_maps = []
    for b in range(8):
        m = dict(shared)
        m["x"] = np.ascontiguousarray(d["x_all"][b])
        in_maps.append(m)

    res = run_bass_kernel_spmd(nc, in_maps, core_ids=list(range(8)))
    out = np.stack([r["y"] for r in res.results], axis=0)
    return out.astype(np.float32)
